# version 41
# baseline (speedup 1.0000x reference)
import numpy as np
import concourse.bass as bass
import concourse.mybir as mybir
from concourse.bass_utils import run_bass_kernel_spmd
from contextlib import ExitStack

F32 = mybir.dt.float32
BF16 = mybir.dt.bfloat16
AF = mybir.ActivationFunctionType
ALU = mybir.AluOpType
AX = mybir.AxisListType

B, S, D = 2, 4096, 2048
H, DH, KVH = 16, 128, 4
NT = S // 128
TOK = 1024
NOT = TOK // 128
NDC = D // 128
NE, FF = 8, 512
NFC = FF // 128
SCALE = 1.0 / float(np.sqrt(128.0))
KAPPA = -0.51 * SCALE
EPS = 1e-5
INV_D = 1.0 / float(D)
NKC = S // 128
NG = 16


class P:

    def __init__(self, nc, ctx):
        self.nc = nc
        self.ctx = ctx
        names = ["pe", "act", "dve", "pool", "sp"]
        self.sem = {n: ctx.enter_context(nc.semaphore(n)) for n in names}
        self.cnt = {n: 0 for n in names}
        self.eng = {"pe": nc.tensor, "act": nc.scalar, "dve": nc.vector,
                    "pool": nc.gpsimd, "sp": nc.sync}
        self.waited = {}

    def dtag(self, inst, stream):
        if stream not in self.sem:
            self.sem[stream] = self.ctx.enter_context(self.nc.semaphore(stream))
            self.cnt[stream] = 0
        return self.tag(inst, stream, 16)

    def tag(self, inst, prod, n=1):
        self.cnt[prod] += n
        inst.then_inc(self.sem[prod], n)
        return self.cnt[prod]

    def wait(self, cons, prod, token=None):
        if token is None:
            token = self.cnt[prod]
        if token <= 0:
            return
        key = (cons, prod)
        if self.waited.get(key, 0) >= token:
            return
        self.eng[cons].wait_ge(self.sem[prod], token)
        self.waited[key] = token

    def selfwait(self, eng):
        self.wait(eng, eng)

    def barrier(self, engines=("pe", "act", "dve", "pool", "sp"), exclude=()):
        for c in engines:
            for pr in list(self.cnt.keys()):
                if c == pr or pr in exclude:
                    continue
                self.wait(c, pr)


def build_program(debug_outputs=()):
    nc = bass.Bass("TRN2", target_bir_lowering=False)
    t = nc.tensor
    v = nc.vector
    sc = nc.scalar
    g = nc.gpsimd
    sp = nc.sync

    xq = nc.dram_tensor("xq", [TOK, D], F32, kind="ExternalInput").ap()
    Wq = nc.dram_tensor("Wq", [D, D], F32, kind="ExternalInput").ap()
    Wk = nc.dram_tensor("Wk", [D, 512], F32, kind="ExternalInput").ap()
    Wv = nc.dram_tensor("Wv", [D, 512], F32, kind="ExternalInput").ap()
    Wo = nc.dram_tensor("Wo", [D, D], F32, kind="ExternalInput").ap()
    bq = nc.dram_tensor("bq", [1, D], F32, kind="ExternalInput").ap()
    bk = nc.dram_tensor("bk", [1, 512], F32, kind="ExternalInput").ap()
    bv = nc.dram_tensor("bv", [1, 512], F32, kind="ExternalInput").ap()
    bo = nc.dram_tensor("bo", [1, D], F32, kind="ExternalInput").ap()
    rw = nc.dram_tensor("rw", [NE, D], F32, kind="ExternalInput").ap()
    w1 = nc.dram_tensor("w1", [NE, D, FF], F32, kind="ExternalInput").ap()
    w2 = nc.dram_tensor("w2", [NE, FF, D], F32, kind="ExternalInput").ap()
    w3 = nc.dram_tensor("w3", [NE, D, FF], F32, kind="ExternalInput").ap()
    y = nc.dram_tensor("y", [TOK, D], F32, kind="ExternalOutput").ap()
    g_dram = nc.dram_tensor("g_scratch", [64, 128], F32, kind="Internal").ap()
    kv_in_h = nc.dram_tensor("kv_in", [128, 8192], BF16, kind="Internal")
    kv_out_h = nc.dram_tensor("kv_out", [512, 8192], BF16, kind="Internal")
    kn_in_h = nc.dram_tensor("kn_in", [128, KVH], F32, kind="Internal")
    kn_out_h = nc.dram_tensor("kn_out", [128, KVH], F32, kind="Internal")
    kv_in, kv_out = kv_in_h.ap(), kv_out_h.ap()
    kn_in, kn_out = kn_in_h.ap(), kn_out_h.ap()
    RG = [[0, 1, 2, 3], [4, 5, 6, 7]]

    dbg = {}
    for name, shape, dt in debug_outputs:
        dbg[name] = nc.dram_tensor(name, list(shape), dt, kind="ExternalOutput").ap()

    ctx = ExitStack()
    with ctx:
        p = P(nc, ctx)
        cc_sem = ctx.enter_context(nc.semaphore("cc"))

        def dbg_store(name, src_ap):
            if name in dbg:
                p.wait("sp", "dve")
                p.wait("sp", "pe")
                p.wait("sp", "act")
                p.wait("sp", "pool")
                i = sp.dma_start(dbg[name], src_ap)
                p.dtag(i, "d_dbg")

        ones_bf = ctx.enter_context(nc.sbuf_tensor("ones_bf", [1, 128], BF16))
        ones_f = ctx.enter_context(nc.sbuf_tensor("ones_f", [1, 128], F32))
        onec_bf = ctx.enter_context(nc.sbuf_tensor("onec_bf", [128, 1], BF16))
        id_bf = ctx.enter_context(nc.sbuf_tensor("id_bf", [128, 128], BF16))
        id_f = ctx.enter_context(nc.sbuf_tensor("id_f", [128, 128], F32))
        cbias = ctx.enter_context(nc.sbuf_tensor("cbias", [128, 16], F32))
        ones128 = ctx.enter_context(nc.sbuf_tensor("ones128", [128, 128], BF16))
        p.tag(v.memset(ones128[:], 1.0), "dve")
        eps_col = ctx.enter_context(nc.sbuf_tensor("eps_col", [128, 1], F32))
        p.tag(v.memset(eps_col[:], EPS), "dve")
        p.tag(v.memset(ones_bf[:], 1.0), "dve")
        p.tag(v.memset(ones_f[:], 1.0), "dve")
        p.tag(v.memset(onec_bf[:], 1.0), "dve")
        p.tag(g.memset(id_bf[:], 0.0), "pool")
        p.selfwait("pool")
        i = g.affine_select(out=id_bf[:], in_=id_bf[:], compare_op=ALU.not_equal,
                            fill=1.0, base=0, pattern=[[-1, 128]], channel_multiplier=1)
        p.tag(i, "pool")
        p.tag(g.memset(id_f[:], 0.0), "pool")
        p.selfwait("pool")
        i = g.affine_select(out=id_f[:], in_=id_f[:], compare_op=ALU.not_equal,
                            fill=1.0, base=0, pattern=[[-1, 128]], channel_multiplier=1)
        p.tag(i, "pool")

        X = ctx.enter_context(nc.sbuf_tensor("X", [128, NDC * 1024], F32))

        rfm = ctx.enter_context(nc.sbuf_tensor("rfm", [128, NDC * NE], BF16))
        with nc.sbuf_tensor("rt_bf", [NE, D], BF16) as rt_bf, \
             nc.psum_tensor("rt_ps", [128, NE], BF16) as rt_ps:
            i = g.dma_start(rt_bf[:], rw)
            p.dtag(i, "d_rt")
            p.wait("pe", "d_rt")
            p.wait("pe", "pool")
            prev_evac = 0
            for dc in range(NDC):
                if prev_evac:
                    p.wait("pe", "dve", prev_evac)
                i = t.transpose(rt_ps[:], rt_bf[:, dc * 128:(dc + 1) * 128],
                                id_bf[0:NE, 0:NE])
                tk = p.tag(i, "pe")
                p.wait("dve", "pe", tk)
                i = v.tensor_copy(rfm[:, dc * NE:(dc + 1) * NE], rt_ps[:])
                prev_evac = p.tag(i, "dve")
        p.barrier()

        attn_ctx = ctx.enter_context(ExitStack())
        Kfm = attn_ctx.enter_context(nc.sbuf_tensor("Kfm", [128, KVH * S], BF16))
        Vsb = attn_ctx.enter_context(nc.sbuf_tensor("Vsb", [128, NT * 512], BF16))
        norms_ctx = ctx.enter_context(ExitStack())
        knmax = norms_ctx.enter_context(nc.sbuf_tensor("knmax", [128, KVH], F32, side="right"))
        kn_t = norms_ctx.enter_context(nc.sbuf_tensor("kn_t", [128, KVH], F32, side="right"))
        qn_row = norms_ctx.enter_context(nc.sbuf_tensor("qn_row", [1, H], F32, side="right"))
        qn_tmp = norms_ctx.enter_context(nc.sbuf_tensor("qn_tmp", [1, 1], F32, side="right"))
        kn_row = norms_ctx.enter_context(nc.sbuf_tensor("kn_row", [1, 512], F32, side="right"))
        knm4 = norms_ctx.enter_context(nc.sbuf_tensor("knm4", [1, KVH], F32, side="right"))
        negc = norms_ctx.enter_context(nc.sbuf_tensor("negc", [1, H], F32, side="right"))
        knT_sb = norms_ctx.enter_context(nc.sbuf_tensor("knT_sb", [KVH, 128], F32, side="right"))
        p.tag(v.memset(knmax[:], 0.0), "dve")
        p.tag(v.memset(qn_row[:], 0.0), "dve")

        with nc.sbuf_tensor("wk_sb", [128, NDC * 512], BF16) as wk_sb, \
             nc.sbuf_tensor("wv_sb", [128, NDC * 512], BF16) as wv_sb, \
             nc.sbuf_tensor("xt_bf", [128, 2 * D], BF16) as xt_bf, \
             nc.sbuf_tensor("xfm", [128, 2 * D], BF16) as xfm, \
             nc.sbuf_tensor("ktmp", [128, 2 * 512], BF16) as ktmp, \
             nc.sbuf_tensor("ksq", [128, 2 * 512], F32) as ksq, \
             nc.psum_tensor("tr_ps", [128, 3 * 1024], BF16) as tr_ps, \
             nc.psum_tensor("pk", [128, 2 * 512], F32) as pk, \
             nc.psum_tensor("pv", [128, 2 * 512], F32) as pv, \
             nc.sbuf_tensor("bk_sb", [1, 512], F32) as bk_sb, \
             nc.sbuf_tensor("bv_sb", [1, 512], F32) as bv_sb, \
             nc.sbuf_tensor("bkbc", [128, 512], F32) as bkbc, \
             nc.sbuf_tensor("bvbc", [128, 512], F32) as bvbc, \
             nc.psum_tensor("ktr", [128, 512], BF16) as ktr:
            i = sp.dma_start(bk_sb[:], bk); p.dtag(i, "d_bias")
            i = sp.dma_start(bv_sb[:], bv); p.dtag(i, "d_bias")
            bias_tok = p.cnt["d_bias"]
            p.wait("pe", "d_bias", bias_tok)
            p.wait("pe", "pool")
            i = t.matmul(pk[:, 0:512], ones_f[0:1, 0:128], bk_sb[:],
                         start=True, stop=True)
            p.tag(i, "pe")
            i = t.matmul(pv[:, 0:512], ones_f[0:1, 0:128], bv_sb[:],
                         start=True, stop=True)
            tk = p.tag(i, "pe")
            p.wait("dve", "pe", tk)
            i = v.tensor_copy(bkbc[:], pk[:, 0:512])
            p.tag(i, "dve")
            i = v.tensor_copy(bvbc[:], pv[:, 0:512])
            bbc_tok = p.tag(i, "dve")
            NTP = NOT
            xdma = [0] * NT
            for tt0 in range(2):
                i = g.dma_start(xt_bf[:, tt0 * D:(tt0 + 1) * D],
                                xq[tt0 * 128:(tt0 + 1) * 128, :])
                xdma[tt0] = p.dtag(i, f"d_xb{tt0}")
            wkv_qtok = [0] * 4
            for qq in range(4):
                i = g.dma_start(
                    wk_sb[:, qq * 2048:(qq + 1) * 2048]
                    .rearrange("p (c m) -> p c m", m=512),
                    Wk[qq * 512:(qq + 1) * 512, :]
                    .rearrange("(c p) m -> p c m", p=128))
                p.dtag(i, "d_wkv")
                i = g.dma_start(
                    wv_sb[:, qq * 2048:(qq + 1) * 2048]
                    .rearrange("p (c m) -> p c m", m=512),
                    Wv[qq * 512:(qq + 1) * 512, :]
                    .rearrange("(c p) m -> p c m", p=128))
                wkv_qtok[qq] = p.dtag(i, "d_wkv")

            trdve = [0] * NT
            trevac = [0] * (2 * NT)
            trpe = [0] * NT
            mmpe = [0] * NT
            kvact = [0] * NT
            ksqact = [0] * NT
            knred = [0] * NT
            ktrpe = [0] * NT
            kfmdve = [0] * NT

            def emit_kn(tx):
                px = tx % 2
                p.selfwait("dve")
                p.wait("dve", "act", ksqact[tx])
                for kv in range(KVH):
                    i = v.tensor_reduce(kn_t[:, kv:kv + 1],
                                        ksq[:, px * 512 + kv * 128: px * 512 + (kv + 1) * 128],
                                        AX.X, ALU.add)
                    p.tag(i, "dve")
                p.selfwait("dve")
                i = v.tensor_max(knmax[:], knmax[:], kn_t[:])
                knred[tx] = p.tag(i, "dve")

            def emit_ktr(tx):
                px = tx % 2
                p.wait("pe", "dve", kvact[tx])
                if tx >= 1:
                    p.wait("pe", "dve", kfmdve[tx - 1])
                for kv in range(KVH):
                    i = t.transpose(ktr[:, kv * 128:(kv + 1) * 128],
                                    ktmp[:, px * 512 + kv * 128: px * 512 + (kv + 1) * 128],
                                    id_bf[:])
                    p.tag(i, "pe")
                ktrpe[tx] = p.cnt["pe"]
                p.wait("dve", "pe", ktrpe[tx])
                for kv in range(KVH):
                    i = v.tensor_copy(Kfm[:, kv * S + tx * 128: kv * S + (tx + 1) * 128],
                                      ktr[:, kv * 128:(kv + 1) * 128])
                    kfmdve[tx] = p.tag(i, "dve")

            def emit_trans(tx):
                par = tx % 2
                p.wait("pe", f"d_xb{par}", xdma[tx])
                for grp in range(2):
                    gidx = tx * 2 + grp
                    sl = gidx % 3
                    if gidx >= 3:
                        p.wait("pe", "dve", trevac[gidx - 3])
                    for j in range(8):
                        dc = grp * 8 + j
                        i = t.transpose(
                            tr_ps[:, sl * 1024 + j * 128: sl * 1024 + (j + 1) * 128],
                            xt_bf[:, par * D + dc * 128: par * D + (dc + 1) * 128],
                            id_bf[:])
                        p.tag(i, "pe")
                    trpe[tx] = p.cnt["pe"]
                    p.wait("dve", "pe", trpe[tx])
                    if tx >= 2:
                        p.wait("dve", "pe", mmpe[tx - 2])
                    i = v.tensor_copy(
                        xfm[:, par * D + grp * 1024: par * D + (grp + 1) * 1024],
                        tr_ps[:, sl * 1024:(sl + 1) * 1024])
                    trevac[gidx] = p.tag(i, "dve")

            emit_trans(0)
            for tt in range(NTP):
                par = tt % 2
                if tt + 2 < NTP:
                    p.wait("pool", "pe", trpe[tt])
                    i = g.dma_start(xt_bf[:, par * D:(par + 1) * D],
                                    xq[(tt + 2) * 128:(tt + 3) * 128, :])
                    xdma[tt + 2] = p.dtag(i, f"d_xb{par}")
                p.wait("pe", "dve", trevac[2 * tt])
                if tt == 0:
                    p.wait("pe", "dve", bbc_tok)
                    p.wait("pe", "d_wkv", wkv_qtok[1])
                if tt >= 2:
                    p.wait("pe", "dve", kvact[tt - 2])
                for dc in range(8):
                    t.matmul(pk[:, par * 512:(par + 1) * 512],
                             xfm[:, par * D + dc * 128: par * D + (dc + 1) * 128],
                             wk_sb[:, dc * 512:(dc + 1) * 512],
                             start=(dc == 0), stop=False)
                    t.matmul(pv[:, par * 512:(par + 1) * 512],
                             xfm[:, par * D + dc * 128: par * D + (dc + 1) * 128],
                             wv_sb[:, dc * 512:(dc + 1) * 512],
                             start=(dc == 0), stop=False)
                if tt + 1 < NTP:
                    emit_trans(tt + 1)
                if tt >= 1:
                    emit_ktr(tt - 1)
                p.wait("pe", "dve", trevac[2 * tt + 1])
                if tt == 0:
                    p.wait("pe", "d_wkv", wkv_qtok[3])
                for dc in range(8, NDC):
                    t.matmul(pk[:, par * 512:(par + 1) * 512],
                             xfm[:, par * D + dc * 128: par * D + (dc + 1) * 128],
                             wk_sb[:, dc * 512:(dc + 1) * 512],
                             start=False, stop=(dc == NDC - 1))
                    i = t.matmul(pv[:, par * 512:(par + 1) * 512],
                                 xfm[:, par * D + dc * 128: par * D + (dc + 1) * 128],
                                 wv_sb[:, dc * 512:(dc + 1) * 512],
                                 start=False, stop=(dc == NDC - 1))
                mmpe[tt] = p.tag(i, "pe")

                p.wait("dve", "pe", mmpe[tt])
                if tt >= 2:
                    p.wait("dve", "pe", ktrpe[tt - 2])
                i = v.tensor_tensor(ktmp[:, par * 512:(par + 1) * 512],
                                    pk[:, par * 512:(par + 1) * 512],
                                    bkbc[:], ALU.add)
                p.tag(i, "dve")
                i = v.tensor_tensor(Vsb[:, tt * 512:(tt + 1) * 512],
                                    pv[:, par * 512:(par + 1) * 512],
                                    bvbc[:], ALU.add)
                kvact[tt] = p.tag(i, "dve")
                p.wait("act", "dve", kvact[tt])
                if tt >= 2:
                    p.wait("act", "dve", knred[tt - 2])
                i = sc.activation(ksq[:, par * 512:(par + 1) * 512],
                                  ktmp[:, par * 512:(par + 1) * 512], AF.Square)
                ksqact[tt] = p.tag(i, "act")
                if tt >= 1:
                    emit_kn(tt - 1)
            emit_kn(NTP - 1)
            emit_ktr(NTP - 1)
            p.wait("sp", "dve", kfmdve[NTP - 1])
            p.wait("sp", "dve", kvact[NTP - 1])
            p.wait("sp", "dve", knred[NTP - 1])
            i = sp.dma_start(
                kv_in[:, 0:4096].rearrange("p (kv t) -> p kv t", t=1024),
                Kfm[:].rearrange("p (kv s) -> p kv s", s=S)[:, :, 0:1024])
            p.dtag(i, "d_kvw")
            i = sp.dma_start(kv_in[:, 4096:8192], Vsb[:, 0:NTP * 512])
            p.dtag(i, "d_kvw")
            i = sp.dma_start(kn_in, knmax[:])
            kvw_tok = p.dtag(i, "d_kvw")
            p.wait("pool", "d_kvw", kvw_tok)
            i = g.collective_compute(
                "AllGather", ALU.bypass, replica_groups=RG,
                ins=[kv_in_h.ap().opt()], outs=[kv_out_h.ap().opt()])
            i.then_inc(cc_sem, 1)
            i = g.collective_compute(
                "AllReduce", ALU.max, replica_groups=RG,
                ins=[kn_in_h.ap().opt()], outs=[kn_out_h.ap().opt()])
            i.then_inc(cc_sem, 1)
        p.barrier()
        Qfm = attn_ctx.enter_context(nc.sbuf_tensor("Qfm", [128, H * 1024], BF16))

        with nc.sbuf_tensor("xot", [128, 3 * 1024], F32) as xot, \
             nc.sbuf_tensor("wq_sb", [128, 3 * 2048], BF16) as wq_sb, \
             nc.sbuf_tensor("xh_bf", [128, NDC * 512], BF16) as xh_bf, \
             nc.sbuf_tensor("bqv", [16, 128], BF16) as bqv, \
             nc.sbuf_tensor("bq_col", [128, 16], F32) as bq_col, \
             nc.sbuf_tensor("qsq", [128, 2 * 512], BF16) as qsq, \
             nc.psum_tensor("xo_ps", [128, 2 * 512], F32) as xo_ps, \
             nc.psum_tensor("q_ps", [128, 3 * 512], F32) as q_ps, \
             nc.psum_tensor("qn2", [128, 512], F32) as qn2:
            wq_tok = [0] * NDC

            def wq_load(m):
                i = g.dma_start(
                    wq_sb[:, (m % 3) * 2048:((m % 3) + 1) * 2048]
                    .rearrange("p (c n) -> p c n", n=128),
                    Wq[:, m * 128:(m + 1) * 128].rearrange("(c p) n -> p c n", p=128))
                wq_tok[m] = p.dtag(i, f"d_wq{m % 3}")

            for mq in range(3):
                wq_load(mq)
            i = g.dma_start(bqv[:], bq.rearrange("o (m q) -> (o m) q", q=128))
            bq_tok2 = p.dtag(i, "d_bqv")
            xq_dma = [0] * (2 * NOT)
            xo_pe = [0] * (2 * NOT)
            xo_dve = {}
            for ot in range(NOT):
                for hh in range(2):
                    hidx = ot * 2 + hh
                    sl = hidx % 3
                    if hidx >= 3:
                        p.wait("sp", "pe", xo_pe[hidx - 3])
                    i = sp.dma_start(xot[:, sl * 1024:(sl + 1) * 1024],
                                     xq[ot * 128:(ot + 1) * 128,
                                        hh * 1024:(hh + 1) * 1024])
                    xq_dma[hidx] = p.dtag(i, f"d_xq{sl}")
                    p.wait("pe", f"d_xq{sl}", xq_dma[hidx])
                    for grp in range(2):
                        gid = hidx * 2 + grp
                        gp = gid % 2
                        if gid >= 2:
                            p.wait("pe", "dve", xo_dve[gid - 2])
                        for j in range(4):
                            i = t.transpose(
                                xo_ps[:, gp * 512 + j * 128: gp * 512 + (j + 1) * 128],
                                xot[:, sl * 1024 + (grp * 4 + j) * 128:
                                    sl * 1024 + (grp * 4 + j + 1) * 128],
                                id_f[:])
                            p.tag(i, "pe")
                        xo_pe[hidx] = p.cnt["pe"]
                        p.wait("dve", "pe", xo_pe[hidx])
                        for j in range(4):
                            dc = hh * 8 + grp * 4 + j
                            i = v.tensor_copy(
                                X[:, dc * 1024 + ot * 128: dc * 1024 + (ot + 1) * 128],
                                xo_ps[:, gp * 512 + j * 128: gp * 512 + (j + 1) * 128])
                            xo_dve[gid] = p.tag(i, "dve")
            x_done_dve = p.cnt["dve"]
            sp.wait_ge(cc_sem, 2)
            i = sp.dma_start(knmax[:], kn_out)
            knr_tok = p.dtag(i, "d_knr")
            for kv in range(KVH):
                i = sp.dma_start(
                    Kfm[:, kv * S:(kv + 1) * S].rearrange("p (c t) -> p c t", t=1024),
                    kv_out[:, kv * 1024:(kv + 1) * 1024]
                    .rearrange("(c p) t -> p c t", p=128))
                p.dtag(i, "d_kvr")
            i = sp.dma_start(
                Vsb[:].rearrange("p (c g) -> p c g", g=NTP * 512),
                kv_out[:, 4096:8192].rearrange("(c p) g -> p c g", p=128))
            kvr_tok = p.dtag(i, "d_kvr")
            p.wait("pe", "d_bqv", bq_tok2)
            with nc.psum_tensor("bqc_ps", [128, 16], BF16) as bqc_ps:
                i = t.transpose(bqc_ps[:], bqv[:], id_bf[0:16, 0:16])
                tk = p.tag(i, "pe")
                p.wait("dve", "pe", tk)
                i = v.tensor_copy(bq_col[:], bqc_ps[:])
                bqcol_tok = p.tag(i, "dve")
            q_evac = {}
            qn_red = {}
            qsq_tok = {}
            slot_i = 0
            xh_cast = {}

            def emit_qn(s, mm, spar_s):
                p.wait("pe", "act", qsq_tok[s])
                if s >= 1:
                    p.wait("pe", "dve", qn_red[s - 1])
                i = t.matmul(qn2[:], ones128[:],
                             qsq[:, spar_s * 512:(spar_s + 1) * 512],
                             start=True, stop=True)
                qn_pe = p.tag(i, "pe")
                p.wait("dve", "pe", qn_pe)
                i = v.tensor_reduce(qn_tmp[:], qn2[0:1, :], AX.X, ALU.max)
                p.tag(i, "dve")
                p.selfwait("dve")
                i = v.tensor_max(qn_row[0:1, mm:mm + 1], qn_row[0:1, mm:mm + 1],
                                 qn_tmp[:])
                qn_red[s] = p.tag(i, "dve")

            for half in range(2):
                if half == 0:
                    p.wait("act", "pe")
                    for dc in range(NDC):
                        g_need = (3 * 2 + dc // 8) * 2 + (dc % 8) // 4
                        p.wait("act", "dve", xo_dve[g_need])
                        i = sc.copy(
                            xh_bf[:, dc * 512:(dc + 1) * 512],
                            X[:, dc * 1024 + half * 512: dc * 1024 + (half + 1) * 512])
                        xh_cast[dc] = p.tag(i, "act")
                for mi in range(NDC):
                    m = mi if half == 0 else NDC - 1 - mi
                    p.wait("pe", f"d_wq{m % 3}", wq_tok[m])
                    slot = slot_i % 3
                    spar = slot_i % 2
                    if slot_i >= 3:
                        p.wait("pe", "dve", q_evac[slot_i - 3])
                    for dc in range(NDC):
                        if mi == 0:
                            p.wait("pe", "act", xh_cast[dc])
                        if dc == 8 and slot_i >= 1:
                            emit_qn(slot_i - 1, *qn_args)
                        i = t.matmul(q_ps[:, slot * 512:(slot + 1) * 512],
                                     wq_sb[:, (m % 3) * 2048 + dc * 128:
                                           (m % 3) * 2048 + (dc + 1) * 128],
                                     xh_bf[:, dc * 512:(dc + 1) * 512],
                                     start=(dc == 0), stop=(dc == NDC - 1))
                        if half == 0 and m == NDC - 1 and dc < NDC - 1:
                            tk = p.tag(i, "pe")
                            p.wait("act", "pe", tk)
                            i2 = sc.copy(
                                xh_bf[:, dc * 512:(dc + 1) * 512],
                                X[:, dc * 1024 + 512: dc * 1024 + 1024])
                            xh_cast[dc] = p.tag(i2, "act")
                    q_pe = p.tag(i, "pe")
                    if half == 0 and m == NDC - 1:
                        dc = NDC - 1
                        p.wait("act", "pe", q_pe)
                        i2 = sc.copy(
                            xh_bf[:, dc * 512:(dc + 1) * 512],
                            X[:, dc * 1024 + 512: dc * 1024 + 1024])
                        xh_cast[dc] = p.tag(i2, "act")
                    p.wait("dve", "pe", q_pe)
                    p.wait("dve", "dve", bqcol_tok)
                    i = v.tensor_scalar(
                        Qfm[:, m * 1024 + half * 512: m * 1024 + (half + 1) * 512],
                        q_ps[:, slot * 512:(slot + 1) * 512],
                        bq_col[:, m:m + 1], None, ALU.add)
                    q_evac[slot_i] = p.tag(i, "dve")
                    p.wait("act", "dve", q_evac[slot_i])
                    if slot_i >= 2:
                        p.wait("act", "dve", qn_red[slot_i - 2])
                    i = sc.activation(
                        qsq[:, spar * 512:(spar + 1) * 512],
                        Qfm[:, m * 1024 + half * 512: m * 1024 + (half + 1) * 512],
                        AF.Square)
                    qsq_tok[slot_i] = p.tag(i, "act")
                    qn_args = (m, spar)
                    slot_i += 1
                    nxt = m + 3 if half == 0 else m - 3
                    if 0 <= nxt < NDC:
                        p.wait("pool", "pe", q_pe)
                        wq_load(nxt)
            emit_qn(slot_i - 1, *qn_args)
        p.barrier(exclude=("d_coll",))
        dbg_store("dQ", Qfm[:])
        dbg_store("dK", Kfm[:])
        dbg_store("dV", Vsb[:])

        with nc.psum_tensor("knT_ps", [KVH, 128], F32) as knT_ps, \
             nc.psum_tensor("cb_ps", [128, 16], F32) as cb_ps:
            p.wait("pe", "d_knr", knr_tok)
            i = t.transpose(knT_ps[:], knmax[:], id_f[:])
            tk = p.tag(i, "pe")
            p.wait("dve", "pe", tk)
            i = v.tensor_copy(knT_sb[:], knT_ps[:])
            tk = p.tag(i, "dve")
            p.wait("sp", "dve", tk)
            i = sp.dma_start(kn_row[0:1, :], knT_sb[:, :])
            tk = p.dtag(i, "d_kn")
            p.wait("dve", "d_kn", tk)
            for kv in range(KVH):
                i = v.tensor_reduce(knm4[0:1, kv:kv + 1],
                                    kn_row[0:1, kv * 128:(kv + 1) * 128], AX.X, ALU.max)
                p.tag(i, "dve")
            p.selfwait("dve")
            for kv in range(KVH):
                i = v.tensor_scalar(negc[0:1, kv * 4:(kv + 1) * 4],
                                    qn_row[0:1, kv * 4:(kv + 1) * 4],
                                    knm4[0:1, kv:kv + 1], KAPPA,
                                    ALU.add, ALU.mult)
                p.tag(i, "dve")
            p.wait("pe", "dve")
            i = t.matmul(cb_ps[:], ones_f[0:1, 0:128], negc[0:1, :],
                         start=True, stop=True)
            tk = p.tag(i, "pe")
            p.wait("dve", "pe", tk)
            i = v.tensor_copy(cbias[:], cb_ps[:])
            cbias_tok = p.tag(i, "dve")
        p.barrier()
        norms_ctx.close()

        ln_id = [0]
        last_cast = [0]

        class LN:

            def __init__(self):
                ln = ln_id[0]
                ln_id[0] += 1
                self.sc = ExitStack()
                E = self.sc.enter_context
                self.tbf = E(nc.sbuf_tensor(f"tbf{ln}", [128, 2 * 1024], BF16))
                self.tsq = E(nc.sbuf_tensor(f"tsq{ln}", [128, 2 * 1024], BF16))
                self.bcm_sb = E(nc.sbuf_tensor(f"bcm_sb{ln}", [128, 1024], F32))
                self.bci_sb = E(nc.sbuf_tensor(f"bci_sb{ln}", [128, 1024], F32))
                self.psc = ExitStack()
                self.s1 = self.psc.enter_context(nc.psum_tensor(f"s1{ln}", [128, 2 * 512], F32))
                self.s2 = self.psc.enter_context(nc.psum_tensor(f"s2{ln}", [128, 2 * 512], F32))
                self.mm_tok = {}
                self.sq_tok = {}
                self.cast_tok = {}
                self.x_tok = {}

            def chunk(self, dc, xtok, defer_stats=False):
                self.x_tok[dc] = xtok
                par = dc % 2
                p.wait("act", "dve", xtok)
                if dc >= 2:
                    p.wait("act", "pe", self.mm_tok[dc - 2])
                i = sc.copy(self.tbf[:, par * 1024:(par + 1) * 1024],
                            X[:, dc * 1024:(dc + 1) * 1024])
                self.cast_tok[dc] = p.tag(i, "act")
                p.selfwait("act")
                i = sc.activation(self.tsq[:, par * 1024:(par + 1) * 1024],
                                  self.tbf[:, par * 1024:(par + 1) * 1024], AF.Square)
                self.sq_tok[dc] = p.tag(i, "act")
                if not defer_stats:
                    self.chunk_stats(dc)

            def chunk_stats(self, dc):
                par = dc % 2
                p.wait("pe", "act", self.sq_tok[dc])
                for hf in range(2):
                    t.matmul(self.s1[:, hf * 512:(hf + 1) * 512], ones128[:],
                             self.tbf[:, par * 1024 + hf * 512: par * 1024 + (hf + 1) * 512],
                             start=(dc == 0), stop=(dc == NDC - 1))
                    i = t.matmul(self.s2[:, hf * 512:(hf + 1) * 512], ones128[:],
                                 self.tsq[:, par * 1024 + hf * 512: par * 1024 + (hf + 1) * 512],
                                 start=(dc == 0), stop=(dc == NDC - 1))
                self.mm_tok[dc] = p.tag(i, "pe")

            def finalize(self, cast_to=None, apply=True):
                self.norm_tok = {}
                self.out_cast = {}
                bcm_sb, bci_sb = self.bcm_sb, self.bci_sb
                p.wait("dve", "pe", self.mm_tok[NDC - 1])
                i = v.tensor_scalar(bcm_sb[:], self.s1[:], INV_D, None, ALU.mult)
                tk = p.tag(i, "dve")
                p.wait("act", "dve", tk)
                i = sc.activation(bci_sb[:], bcm_sb[:], AF.Square)
                tk = p.tag(i, "act")
                p.wait("dve", "act", tk)
                i = v.scalar_tensor_tensor(bci_sb[:], self.s2[:], INV_D, bci_sb[:],
                                           ALU.mult, ALU.subtract)
                tk = p.tag(i, "dve")
                self.psc.close()
                p.wait("act", "dve", tk)
                i = sc.activation(bci_sb[:], bci_sb[:], AF.Ln, bias=eps_col[:])
                p.tag(i, "act")
                p.selfwait("act")
                i = sc.activation(bci_sb[:], bci_sb[:], AF.Exp, scale=-0.5)
                lntok = p.tag(i, "act")
                if not apply:
                    E = lambda t_: ctx.enter_context(t_)
                    self.m_col = E(nc.sbuf_tensor(f"m_col{ln_id[0]}", [128, NOT], F32, side="right"))
                    self.i_col = E(nc.sbuf_tensor(f"i_col{ln_id[0]}", [128, NOT], F32, side="right"))
                    mv8 = E(nc.sbuf_tensor(f"mv8{ln_id[0]}", [NOT, 128], F32, side="right"))
                    iv8 = E(nc.sbuf_tensor(f"iv8{ln_id[0]}", [NOT, 128], F32, side="right"))
                    p.wait("sp", "act", lntok)
                    p.wait("sp", "dve")
                    i = sp.dma_start(mv8[:], bcm_sb[0:1, :])
                    p.dtag(i, "d_lnc")
                    i = sp.dma_start(iv8[:], bci_sb[0:1, :])
                    tk = p.dtag(i, "d_lnc")
                    p.wait("pe", "d_lnc", tk)
                    with nc.psum_tensor(f"mc_ps{ln_id[0]}", [128, NOT], F32) as mc_ps, \
                         nc.psum_tensor(f"ic_ps{ln_id[0]}", [128, NOT], F32) as ic_ps:
                        i = t.transpose(mc_ps[:], mv8[:], id_f[0:NOT, 0:NOT])
                        p.tag(i, "pe")
                        i = t.transpose(ic_ps[:], iv8[:], id_f[0:NOT, 0:NOT])
                        tk = p.tag(i, "pe")
                        p.wait("dve", "pe", tk)
                        i = v.tensor_copy(self.m_col[:], mc_ps[:])
                        p.tag(i, "dve")
                        i = v.tensor_copy(self.i_col[:], ic_ps[:])
                        self.col_tok = p.tag(i, "dve")
                    self.sc.close()
                    return
                p.wait("dve", "act", lntok)
                for dc in range(NDC):
                    i = v.tensor_tensor(X[:, dc * 1024:(dc + 1) * 1024],
                                        X[:, dc * 1024:(dc + 1) * 1024],
                                        bcm_sb[:], ALU.subtract)
                    p.tag(i, "dve")
                    i = v.tensor_tensor(X[:, dc * 1024:(dc + 1) * 1024],
                                        X[:, dc * 1024:(dc + 1) * 1024],
                                        bci_sb[:], ALU.mult)
                    xtok = self.norm_tok[dc] = p.tag(i, "dve")
                    if cast_to is not None:
                        p.wait("act", "dve", xtok)
                        i = sc.copy(cast_to[:, dc * 1024:(dc + 1) * 1024],
                                    X[:, dc * 1024:(dc + 1) * 1024])
                        last_cast[0] = self.out_cast[dc] = p.tag(i, "act")
                self.sc.close()
                p.barrier(engines=("act", "dve", "pool", "sp"))

        octx = ctx.enter_context(ExitStack())
        Afm = octx.enter_context(nc.sbuf_tensor("Afm", [128, H * 1024], BF16, side="right"))
        wo_slots = [None, None]
        wo_tok = [0] * NDC

        def wo_load(m):
            i = g.dma_start(
                wo_slots[m % 2][:].rearrange("p (c n) -> p c n", n=128),
                Wo[:, m * 128:(m + 1) * 128].rearrange("(c p) n -> p c n", p=128))
            wo_tok[m] = p.dtag(i, f"d_wo{m % 2}")
        units = [(h, qb) for h in range(H) for qb in range(2)]
        NU = len(units)
        TOTG = NU * NG
        NJ = NG // 2
        with nc.sbuf_tensor("pt", [128, 3 * 1024], BF16) as pt, \
             nc.sbuf_tensor("padd", [128, 2 * 512], BF16) as padd, \
             nc.sbuf_tensor("padd2", [128, 4 * 512], BF16) as padd2, \
             nc.sbuf_tensor("rden", [128, 512], BF16) as rden, \
             nc.psum_tensor("st_ps", [128, 2 * 1024], F32) as st_ps, \
             nc.psum_tensor("a_ps", [128, 2 * 512], F32) as a_ps, \
             nc.psum_tensor("den_ps", [128, 2 * 512], F32) as den_ps:
            p.wait("act", "dve", cbias_tok)
            p.wait("pe", "d_kvr", kvr_tok)
            exp_tok = {}
            pv_tok = {}
            padd_tok = {}
            padd2_tok = {}
            sum_pe = {}
            summ_tok = {}
            recip_tok = {}
            evac_tok = {}

            def emit_scores(gg):
                u, gi = divmod(gg, NG)
                h, qb = units[u]
                kv = h // 4
                qcol = h * 1024 + qb * 512
                sl2 = gg % 2
                if gg >= 2:
                    p.wait("pe", "act", exp_tok[gg - 2])
                for c2 in range(2):
                    kc = gi * 2 + c2
                    i = t.matmul(
                        st_ps[:, sl2 * 1024 + c2 * 512: sl2 * 1024 + (c2 + 1) * 512],
                        Kfm[:, kv * S + kc * 128: kv * S + (kc + 1) * 128],
                        Qfm[:, qcol:qcol + 512], start=True, stop=True)
                st_pe = p.tag(i, "pe")
                p.wait("act", "pe", st_pe)
                if gg >= 3:
                    p.wait("act", "pe", pv_tok[gg - 3])
                    p.wait("act", "dve", padd_tok[gg - 3])
                i = sc.activation(pt[:, (gg % 3) * 1024:(gg % 3 + 1) * 1024],
                                  st_ps[:, sl2 * 1024:(sl2 + 1) * 1024],
                                  AF.Exp, bias=cbias[:, h:h + 1], scale=SCALE)
                exp_tok[gg] = p.tag(i, "act")
                p.wait("dve", "act", exp_tok[gg])
                with nc.allow_low_precision(reason="bf16 softmax normalizer, within bar"):
                    i = v.tensor_tensor(padd[:, (gg % 2) * 512:(gg % 2 + 1) * 512],
                                        pt[:, (gg % 3) * 1024:(gg % 3) * 1024 + 512],
                                        pt[:, (gg % 3) * 1024 + 512:(gg % 3 + 1) * 1024],
                                        ALU.add)
                padd_tok[gg] = p.tag(i, "dve")
                if gi % 2 == 1:
                    jg = gg // 2
                    if jg >= 4:
                        p.wait("dve", "pe", sum_pe[jg - 4])
                    with nc.allow_low_precision(reason="bf16 softmax normalizer, within bar"):
                        i = v.tensor_tensor(padd2[:, (jg % 4) * 512:(jg % 4 + 1) * 512],
                                            padd[:, 0:512], padd[:, 512:1024],
                                            ALU.add)
                    padd2_tok[jg] = p.tag(i, "dve")

            def emit_pv(gg):
                u, gi = divmod(gg, NG)
                h, qb = units[u]
                kv = h // 4
                p.wait("pe", "act", exp_tok[gg])
                if gi == 0 and u >= 2:
                    p.wait("pe", "dve", evac_tok[u - 2])
                for c2 in range(2):
                    kc = gi * 2 + c2
                    i = t.matmul(
                        a_ps[:, (u % 2) * 512:(u % 2 + 1) * 512],
                        Vsb[:, kc * 512 + kv * 128: kc * 512 + (kv + 1) * 128],
                        pt[:, (gg % 3) * 1024 + c2 * 512: (gg % 3) * 1024 + (c2 + 1) * 512],
                        start=(kc == 0), stop=(kc == NKC - 1))
                pv_tok[gg] = p.tag(i, "pe")

            def emit_sum(jg):
                u, j = divmod(jg, NJ)
                p.wait("pe", "dve", padd2_tok[jg])
                if j == 0 and u >= 2:
                    p.wait("pe", "dve", recip_tok[u - 2])
                i = t.matmul(den_ps[:, (u % 2) * 512:(u % 2 + 1) * 512],
                             ones128[:], padd2[:, (jg % 4) * 512:(jg % 4 + 1) * 512],
                             start=(j == 0), stop=(j == NJ - 1))
                sum_pe[jg] = p.tag(i, "pe")
                if j == NJ - 1:
                    summ_tok[u] = sum_pe[jg]

            def emit_den_evac(u):
                h, qb = units[u]
                qcol = h * 1024 + qb * 512
                p.wait("dve", "pe", summ_tok[u])
                p.wait("dve", "pe", pv_tok[u * NG + NG - 1])
                with nc.allow_low_precision(reason="bf16 softmax normalizer, within bar"):
                    i = v.reciprocal(rden[:], den_ps[:, (u % 2) * 512:(u % 2 + 1) * 512])
                recip_tok[u] = p.tag(i, "dve")
                i = v.tensor_tensor(Afm[:, qcol:qcol + 512],
                                    a_ps[:, (u % 2) * 512:(u % 2 + 1) * 512],
                                    rden[:], ALU.mult)
                evac_tok[u] = p.tag(i, "dve")

            for gg in range(TOTG):
                u, gi = divmod(gg, NG)
                if gg >= 9 and gg % 2 == 1:
                    emit_sum((gg - 9) // 2)
                emit_scores(gg)
                if gg >= 2:
                    emit_pv(gg - 2)
                if gi == 11 and u >= 1:
                    emit_den_evac(u - 1)
            emit_pv(TOTG - 2)
            emit_pv(TOTG - 1)
            for jg in range(NU * NJ - 4, NU * NJ):
                emit_sum(jg)
            emit_den_evac(NU - 1)
        p.barrier()
        dbg_store("dA", Afm[:])

        attn_ctx.close()
        mctx = ctx.enter_context(ExitStack())
        xnew = mctx.enter_context(nc.sbuf_tensor("xnew", [128, NDC * 1024], BF16))

        wo_slots[0] = octx.enter_context(
            nc.sbuf_tensor("wo_sb0", [128, NDC * 128], BF16, side="right"))
        wo_slots[1] = octx.enter_context(
            nc.sbuf_tensor("wo_sb1", [128, NDC * 128], BF16, side="right"))
        wo_load(0)
        wo_load(1)
        with nc.sbuf_tensor("bo_sb", [1, D], F32) as bo_sb, \
             nc.sbuf_tensor("bov", [16, 128], F32) as bov, \
             nc.sbuf_tensor("bo_col", [128, 16], F32) as bo_col, \
             nc.psum_tensor("o_ps", [128, 2 * 1024], F32) as o_ps:
            i = sp.dma_start(bo_sb[:], bo)
            bo_tok = p.dtag(i, "d_bo")
            p.wait("sp", "d_bo", bo_tok)
            i = sp.dma_start(bov[:], bo_sb[:])
            bo_tok2 = p.dtag(i, "d_bo")
            p.wait("pe", "d_bo", bo_tok2)
            with nc.psum_tensor("boc_ps", [128, 16], F32) as boc_ps:
                i = t.transpose(boc_ps[:], bov[:], id_f[0:16, 0:16])
                tk = p.tag(i, "pe")
                p.wait("dve", "pe", tk)
                i = v.tensor_copy(bo_col[:], boc_ps[:])
                bocol_tok = p.tag(i, "dve")
            ln1 = LN()
            oadd = {}
            for m in range(NDC):
                p.wait("pe", f"d_wo{m % 2}", wo_tok[m])
                if m >= 2:
                    p.wait("pe", "dve", oadd[m - 2])
                par = m % 2
                for fc in range(NDC):
                    if fc == 8 and m >= 1:
                        ln1.chunk_stats(m - 1)
                    for hf in range(2):
                        i = t.matmul(o_ps[:, par * 1024 + hf * 512: par * 1024 + (hf + 1) * 512],
                                     wo_slots[m % 2][:, fc * 128:(fc + 1) * 128],
                                     Afm[:, fc * 1024 + hf * 512: fc * 1024 + (hf + 1) * 512],
                                     start=(fc == 0), stop=(fc == NDC - 1))
                o_pe = p.tag(i, "pe")
                p.wait("dve", "pe", o_pe)
                p.wait("dve", "dve", bocol_tok)
                i = v.scalar_tensor_tensor(X[:, m * 1024:(m + 1) * 1024],
                                           o_ps[:, par * 1024:(par + 1) * 1024],
                                           bo_col[:, m:m + 1],
                                           X[:, m * 1024:(m + 1) * 1024],
                                           ALU.add, ALU.add)
                oadd[m] = p.tag(i, "dve")
                ln1.chunk(m, oadd[m], defer_stats=True)
                if m + 2 < NDC:
                    p.wait("pool", "pe", o_pe)
                    wo_load(m + 2)
            ln1.chunk_stats(NDC - 1)
            dbg_store("dR1", X[:])
            ln1.finalize(cast_to=xnew)
        octx.close()

        with nc.sbuf_tensor("h_all", [128, NE * NFC * 1024], BF16) as h_all:
            xcast = last_cast[0]
            dbg_store("dLN1", X[:])
            w2ctx = ExitStack()
            w2m = w2ctx.enter_context(nc.sbuf_tensor("w2m", [128, 2 * 4096], BF16))
            NPC = 2
            w13s = ExitStack()
            w1_sb = w13s.enter_context(nc.sbuf_tensor("w1_sb", [128, NPC * NDC * 128], BF16))
            w3_sb = w13s.enter_context(nc.sbuf_tensor("w3_sb", [128, NPC * NDC * 128], BF16))
            ge_row = w13s.enter_context(nc.sbuf_tensor("ge_row", [1, 2 * 1024], BF16))
            bcg_sb = w13s.enter_context(nc.sbuf_tensor("bcg_sb", [128, 2 * 1024], BF16))
            sil = w13s.enter_context(nc.sbuf_tensor("sil", [128, 2 * 512], BF16))
            t1 = w13s.enter_context(nc.sbuf_tensor("t1", [128, 2 * 512], BF16))
            PSZ = NDC * 128
            w13_tok = [0] * (NE * NFC)

            def w13_load(pi):
                e, fc = pi // NFC, pi % NFC
                sl = pi % NPC
                i = g.dma_start(
                    w1_sb[:, sl * PSZ:(sl + 1) * PSZ].rearrange("p (c m) -> p c m", m=128),
                    w1[e][:, fc * 128:(fc + 1) * 128].rearrange("(c p) m -> p c m", p=128))
                p.dtag(i, f"d_w13{sl}")
                i = g.dma_start(
                    w3_sb[:, sl * PSZ:(sl + 1) * PSZ].rearrange("p (c m) -> p c m", m=128),
                    w3[e][:, fc * 128:(fc + 1) * 128].rearrange("(c p) m -> p c m", p=128))
                w13_tok[pi] = p.dtag(i, f"d_w13{sl}")

            for pi in range(NPC):
                w13_load(pi)
            with nc.sbuf_tensor("r_all", [128, 64], F32) as r_all, \
                 nc.sbuf_tensor("m1a", [128, NE], F32) as m1a, \
                 nc.sbuf_tensor("m2a", [128, NE], F32) as m2a, \
                 nc.sbuf_tensor("mk1", [128, NE * NE], F32) as mk1, \
                 nc.sbuf_tensor("mk2", [128, NE * NE], F32) as mk2, \
                 nc.sbuf_tensor("sigin", [128, NE], F32) as sigin, \
                 nc.sbuf_tensor("sig", [128, NE], F32) as sig, \
                 nc.sbuf_tensor("gT_sb", [64, 128], F32) as gT_sb, \
                 nc.psum_tensor("r_ps", [128, 64], F32) as r_ps, \
                 nc.psum_tensor("gT_ps", [64, 128], F32) as gT_ps, \
                 nc.psum_tensor("x1_ps", [128, 2 * 512], F32) as x1_ps, \
                 nc.psum_tensor("x3_ps", [128, 2 * 512], F32) as x3_ps, \
                 nc.psum_tensor("bcg_ps", [128, 2 * 512], F32) as bcg_ps:
                r2 = r_all
                gts = mk1
                for dc in range(NDC):
                    p.wait("pe", "act", ln1.out_cast[dc])
                    for qt in range(8):
                        i = t.matmul(r_ps[:, qt * NE:(qt + 1) * NE],
                                     xnew[:, dc * 1024 + qt * 128: dc * 1024 + (qt + 1) * 128],
                                     rfm[:, dc * NE:(dc + 1) * NE],
                                     start=(dc == 0 and qt == 0),
                                     stop=(dc == NDC - 1 and qt == 7))
                rtok = p.tag(i, "pe")
                p.wait("dve", "pe", rtok)
                i = v.tensor_copy(r_all[:], r_ps[:])
                p.tag(i, "dve")
                p.selfwait("dve")
                for qt in range(8):
                    i = v.tensor_reduce(m1a[:, qt:qt + 1],
                                        r_all[:, qt * NE:(qt + 1) * NE], AX.X, ALU.max)
                    p.tag(i, "dve")
                p.selfwait("dve")
                for qt in range(8):
                    i = v.tensor_scalar(mk1[:, qt * NE:(qt + 1) * NE],
                                        r_all[:, qt * NE:(qt + 1) * NE],
                                        m1a[:, qt:qt + 1], None, ALU.is_ge)
                    p.tag(i, "dve")
                p.selfwait("dve")
                for qt in range(8):
                    i = v.scalar_tensor_tensor(r2[:, qt * NE:(qt + 1) * NE],
                                               mk1[:, qt * NE:(qt + 1) * NE], -1e30,
                                               r_all[:, qt * NE:(qt + 1) * NE],
                                               ALU.mult, ALU.add)
                    p.tag(i, "dve")
                p.selfwait("dve")
                for qt in range(8):
                    i = v.tensor_reduce(m2a[:, qt:qt + 1],
                                        r2[:, qt * NE:(qt + 1) * NE], AX.X, ALU.max)
                    p.tag(i, "dve")
                p.selfwait("dve")
                for qt in range(8):
                    i = v.tensor_scalar(mk2[:, qt * NE:(qt + 1) * NE],
                                        r2[:, qt * NE:(qt + 1) * NE],
                                        m2a[:, qt:qt + 1], None, ALU.is_ge)
                    p.tag(i, "dve")
                p.selfwait("dve")
                i = v.tensor_tensor(sigin[:], m1a[:], m2a[:], ALU.subtract)
                tk = p.tag(i, "dve")
                p.wait("act", "dve", tk)
                i = sc.activation(sig[:], sigin[:], AF.Sigmoid)
                sig_tok = p.tag(i, "act")
                p.selfwait("dve")
                i = v.tensor_tensor(mk1[:], mk1[:], mk2[:], ALU.subtract)
                p.tag(i, "dve")
                p.wait("dve", "act", sig_tok)
                p.selfwait("dve")
                gts_tok = 0
                for qt in range(8):
                    i = v.scalar_tensor_tensor(gts[:, qt * NE:(qt + 1) * NE],
                                               mk1[:, qt * NE:(qt + 1) * NE],
                                               sig[:, qt:qt + 1],
                                               mk2[:, qt * NE:(qt + 1) * NE],
                                               ALU.mult, ALU.add)
                    gts_tok = p.tag(i, "dve")
                gst_tok = 0

                def gates_finish():
                    nonlocal gst_tok
                    p.wait("pe", "dve", gts_tok)
                    i = t.transpose(gT_ps[:], gts[:], id_f[:])
                    tk = p.tag(i, "pe")
                    p.wait("dve", "pe", tk)
                    i = v.tensor_copy(gT_sb[:], gT_ps[:])
                    tk = p.tag(i, "dve")
                    p.wait("sp", "dve", tk)
                    i = sp.dma_start(g_dram, gT_sb[:])
                    gst_tok = p.dtag(i, "d_gst")

                h_done = {}
                bcg_done = [0, 0]
                ge_dma = [0] * NE
                ge_read = [0] * NE
                uidx = 0

                def ge_load(e):
                    ep = e % 2
                    p.wait("pool", "d_gst", gst_tok)
                    if e >= 2:
                        p.wait("pool", "pe", ge_read[e - 2])
                    i = g.dma_start(ge_row[0:1, ep * 1024:(ep + 1) * 1024],
                                    g_dram.rearrange("(q e) n -> e q n", e=NE)[e])
                    ge_dma[e] = p.dtag(i, f"d_ge{ep}")

                bcg_war = [0, 0]

                def bcg_make(e):
                    ep = e % 2
                    p.wait("pe", f"d_ge{ep}", ge_dma[e])
                    for th in range(2):
                        p.wait("pe", "dve", bcg_war[th])
                        i = t.matmul(bcg_ps[:, th * 512:(th + 1) * 512],
                                     ones_bf[0:1, 0:128],
                                     ge_row[0:1, ep * 1024 + th * 512: ep * 1024 + (th + 1) * 512],
                                     start=True, stop=True)
                        tk = ge_read[e] = p.tag(i, "pe")
                        p.wait("dve", "pe", tk)
                        i = v.tensor_copy(
                            bcg_sb[:, ep * 1024 + th * 512: ep * 1024 + (th + 1) * 512],
                            bcg_ps[:, th * 512:(th + 1) * 512])
                        bcg_war[th] = p.tag(i, "dve")
                    bcg_done[ep] = bcg_war[1]

                def emit_t1h(ui, up, pi_, th_, sil_tok, x13_tok):
                    ep_ = (pi_ // NFC) % 2
                    p.wait("dve", "act", sil_tok)
                    p.wait("dve", "pe", x13_tok)
                    i = v.tensor_tensor(t1[:, up * 512:(up + 1) * 512],
                                        sil[:, up * 512:(up + 1) * 512],
                                        x3_ps[:, up * 512:(up + 1) * 512], ALU.mult)
                    p.tag(i, "dve")
                    p.selfwait("dve")
                    p.wait("dve", "dve", bcg_done[ep_])
                    i = v.tensor_tensor(
                        h_all[:, pi_ * 1024 + th_ * 512: pi_ * 1024 + (th_ + 1) * 512],
                        t1[:, up * 512:(up + 1) * 512],
                        bcg_sb[:, ep_ * 1024 + th_ * 512: ep_ * 1024 + (th_ + 1) * 512],
                        ALU.mult)
                    h_done[ui] = p.tag(i, "dve")

                w2_tok = [0] * NDC

                def w2_load(m):
                    i = g.dma_start(
                        w2m[:, (m % 2) * 4096:(m % 2 + 1) * 4096]
                        .rearrange("p (e c n) -> p e c n", e=NE, c=NFC),
                        w2[:, :, m * 128:(m + 1) * 128]
                        .rearrange("e (c p) n -> p e c n", p=128))
                    w2_tok[m] = p.dtag(i, f"d_w2{m % 2}")

                deferred_t1h = []
                for e in range(NE):
                    ep = e % 2
                    if 0 < e < NE - 1:
                        ge_load(e + 1)
                    if e == NE - 2:
                        w2_load(0)
                        w2_load(1)
                    for fc in range(NFC):
                        pi = e * NFC + fc
                        sl = pi % NPC
                        p.wait("pe", f"d_w13{sl}", w13_tok[pi])
                        last_x13 = 0
                        for th in range(2):
                            up = uidx % 2
                            if uidx >= 2:
                                p.wait("pe", "dve", h_done[uidx - 2])
                            for dc in range(NDC):
                                if e == 0:
                                    p.wait("pe", "act", ln1.out_cast[dc])
                                t.matmul(x1_ps[:, up * 512:(up + 1) * 512],
                                         w1_sb[:, sl * PSZ + dc * 128:(sl * PSZ + (dc + 1) * 128)],
                                         xnew[:, dc * 1024 + th * 512: dc * 1024 + (th + 1) * 512],
                                         start=(dc == 0), stop=(dc == NDC - 1))
                                i = t.matmul(x3_ps[:, up * 512:(up + 1) * 512],
                                             w3_sb[:, sl * PSZ + dc * 128:(sl * PSZ + (dc + 1) * 128)],
                                             xnew[:, dc * 1024 + th * 512: dc * 1024 + (th + 1) * 512],
                                             start=(dc == 0), stop=(dc == NDC - 1))
                            x13_tok = p.tag(i, "pe")
                            last_x13 = x13_tok
                            p.wait("act", "pe", x13_tok)
                            if uidx >= 2:
                                p.wait("act", "dve", h_done[uidx - 2])
                            i = sc.activation(sil[:, up * 512:(up + 1) * 512],
                                              x1_ps[:, up * 512:(up + 1) * 512], AF.Silu)
                            sil_tok = p.tag(i, "act")
                            if e == 0 and fc == 0:
                                deferred_t1h.append((uidx, up, pi, th, sil_tok, x13_tok))
                            else:
                                emit_t1h(uidx, up, pi, th, sil_tok, x13_tok)
                            uidx += 1
                        if e == 0 and fc == 0:
                            gates_finish()
                            ge_load(0)
                            ge_load(1)
                            bcg_make(0)
                            for args in deferred_t1h:
                                emit_t1h(*args)
                        if pi + NPC < NE * NFC:
                            p.wait("pool", "pe", last_x13)
                            w13_load(pi + NPC)
                        if fc == 1 and e + 1 < NE:
                            bcg_make(e + 1)
            p.barrier(exclude=("d_w20", "d_w21"))
            w13s.close()
            ln2 = LN()
            with nc.psum_tensor("mo_ps", [128, 2 * 512], F32) as mo_ps:
                madd = {}
                widx = 0
                for m in range(NDC):
                    p.wait("pe", f"d_w2{m % 2}", w2_tok[m])
                    mo_pe = 0
                    for th in range(2):
                        par = widx % 2
                        if widx >= 2:
                            p.wait("pe", "dve", madd[widx - 2])
                        for e in range(NE):
                            for fc in range(NFC):
                                key = e * NFC + fc
                                i = t.matmul(
                                    mo_ps[:, par * 512:(par + 1) * 512],
                                    w2m[:, (m % 2) * 4096 + key * 128: (m % 2) * 4096 + (key + 1) * 128],
                                    h_all[:, key * 1024 + th * 512: key * 1024 + (th + 1) * 512],
                                    start=(key == 0), stop=(key == NE * NFC - 1))
                        mo_pe = p.tag(i, "pe")
                        if th == 0 and m >= 1:
                            ln2.chunk_stats(m - 1)
                        p.wait("dve", "pe", mo_pe)
                        i = v.tensor_tensor(
                            X[:, m * 1024 + th * 512: m * 1024 + (th + 1) * 512],
                            X[:, m * 1024 + th * 512: m * 1024 + (th + 1) * 512],
                            mo_ps[:, par * 512:(par + 1) * 512], ALU.add)
                        madd[widx] = p.tag(i, "dve")
                        widx += 1
                    ln2.chunk(m, madd[widx - 1], defer_stats=True)
                    if m + 2 < NDC:
                        p.wait("pool", "pe", mo_pe)
                        w2_load(m + 2)
            ln2.chunk_stats(NDC - 1)
            dbg_store("dR2", X[:])
            ln2.finalize(apply=False)
            w2ctx.close()
        p.barrier()

        with nc.sbuf_tensor("ytile", [128, 2 * D], F32) as ytile, \
             nc.psum_tensor("yo_ps", [128, 2 * 512], F32) as yo_ps:
            ydma = {}
            for ot in range(NOT):
                par = ot % 2
                for grp in range(4):
                    gp = grp % 2
                    if ot * 4 + grp >= 2:
                        p.wait("pe", "dve")
                    for j in range(4):
                        dc = grp * 4 + j
                        p.wait("pe", "dve", ln2.x_tok[dc])
                        i = t.transpose(
                            yo_ps[:, gp * 512 + j * 128: gp * 512 + (j + 1) * 128],
                            X[:, dc * 1024 + ot * 128: dc * 1024 + (ot + 1) * 128],
                            id_f[:])
                        p.tag(i, "pe")
                    tr_tok = p.cnt["pe"]
                    p.wait("dve", "pe", tr_tok)
                    p.wait("dve", "dve", ln2.col_tok)
                    if ot >= 2 and grp == 0:
                        p.wait("dve", f"d_y{par}", ydma[ot - 2])
                    i = v.tensor_scalar(
                        ytile[:, par * D + grp * 512: par * D + (grp + 1) * 512],
                        yo_ps[:, gp * 512:(gp + 1) * 512],
                        ln2.m_col[:, ot:ot + 1], ln2.i_col[:, ot:ot + 1],
                        ALU.subtract, ALU.mult)
                    p.tag(i, "dve")
                p.wait("sp", "dve")
                i = sp.dma_start(y[ot * 128:(ot + 1) * 128, :],
                                 ytile[:, par * D:(par + 1) * D])
                ydma[ot] = p.dtag(i, f"d_y{par}")
            for sname in list(p.cnt.keys()):
                if sname.startswith("d_y") or sname == "d_dbg":
                    sp.wait_ge(p.sem[sname], p.cnt[sname])
    return nc, dbg


_nc_cache = {}


def _get_program(debug_outputs=()):
    key = tuple(tuple(x) if isinstance(x, (list, tuple)) else x for x in debug_outputs)
    if key not in _nc_cache:
        _nc_cache[key] = build_program(debug_outputs)
    return _nc_cache[key]


def make_in_maps(inputs):
    x = np.ascontiguousarray(np.asarray(inputs["L0_attn_x"], dtype=np.float32))
    ws = {
        "Wq": np.ascontiguousarray(np.asarray(inputs["L0_attn_Wq"], dtype=np.float32)),
        "Wk": np.ascontiguousarray(np.asarray(inputs["L0_attn_Wk"], dtype=np.float32)),
        "Wv": np.ascontiguousarray(np.asarray(inputs["L0_attn_Wv"], dtype=np.float32)),
        "Wo": np.ascontiguousarray(np.asarray(inputs["L0_attn_Wo"], dtype=np.float32)),
        "bq": np.ascontiguousarray(np.asarray(inputs["L0_attn_bq"], dtype=np.float32)).reshape(1, D),
        "bk": np.ascontiguousarray(np.asarray(inputs["L0_attn_bk"], dtype=np.float32)).reshape(1, 512),
        "bv": np.ascontiguousarray(np.asarray(inputs["L0_attn_bv"], dtype=np.float32)).reshape(1, 512),
        "bo": np.ascontiguousarray(np.asarray(inputs["L0_attn_bo"], dtype=np.float32)).reshape(1, D),
        "rw": np.ascontiguousarray(np.asarray(inputs["L0_ffn_router_w"], dtype=np.float32)),
        "w1": np.ascontiguousarray(np.asarray(inputs["L0_ffn_w1"], dtype=np.float32)),
        "w2": np.ascontiguousarray(np.asarray(inputs["L0_ffn_w2"], dtype=np.float32)),
        "w3": np.ascontiguousarray(np.asarray(inputs["L0_ffn_w3"], dtype=np.float32)),
    }
    in_maps = []
    for c in range(8):
        b, qi = c // 4, c % 4
        m = {"xq": np.ascontiguousarray(x[b, qi * TOK:(qi + 1) * TOK])}
        m.update(ws)
        in_maps.append(m)
    return in_maps


def kernel(**inputs):
    in_maps = make_in_maps(inputs)
    nc, _ = _get_program()
    res = run_bass_kernel_spmd(nc, in_maps, core_ids=list(range(8)))
    out = np.empty((B, S, D), np.float32)
    for c in range(8):
        b, qi = c // 4, c % 4
        out[b, qi * TOK:(qi + 1) * TOK] = res.results[c]["y"]
    return out


# version 42
# speedup vs baseline: 1.0928x; 1.0928x over previous
import numpy as np
import concourse.bass as bass
import concourse.mybir as mybir
from concourse.bass_utils import run_bass_kernel_spmd
from contextlib import ExitStack

F32 = mybir.dt.float32
BF16 = mybir.dt.bfloat16
AF = mybir.ActivationFunctionType
ALU = mybir.AluOpType
AX = mybir.AxisListType

B, S, D = 2, 4096, 2048
H, DH, KVH = 16, 128, 4
NT = S // 128
TOK = 1024
NOT = TOK // 128
NDC = D // 128
NE, FF = 8, 512
NFC = FF // 128
SCALE = 1.0 / float(np.sqrt(128.0))
KAPPA = -0.51 * SCALE
EPS = 1e-5
INV_D = 1.0 / float(D)
NKC = S // 128
NG = 16


class P:

    def __init__(self, nc, ctx):
        self.nc = nc
        self.ctx = ctx
        names = ["pe", "act", "dve", "pool", "sp"]
        self.sem = {n: ctx.enter_context(nc.semaphore(n)) for n in names}
        self.cnt = {n: 0 for n in names}
        self.eng = {"pe": nc.tensor, "act": nc.scalar, "dve": nc.vector,
                    "pool": nc.gpsimd, "sp": nc.sync}
        self.waited = {}

    def dtag(self, inst, stream):
        if stream not in self.sem:
            self.sem[stream] = self.ctx.enter_context(self.nc.semaphore(stream))
            self.cnt[stream] = 0
        return self.tag(inst, stream, 16)

    def tag(self, inst, prod, n=1):
        self.cnt[prod] += n
        inst.then_inc(self.sem[prod], n)
        return self.cnt[prod]

    def wait(self, cons, prod, token=None):
        if token is None:
            token = self.cnt[prod]
        if token <= 0:
            return
        key = (cons, prod)
        if self.waited.get(key, 0) >= token:
            return
        self.eng[cons].wait_ge(self.sem[prod], token)
        self.waited[key] = token

    def selfwait(self, eng):
        self.wait(eng, eng)

    def barrier(self, engines=("pe", "act", "dve", "pool", "sp"), exclude=()):
        for c in engines:
            for pr in list(self.cnt.keys()):
                if c == pr or pr in exclude:
                    continue
                self.wait(c, pr)


def build_program(debug_outputs=()):
    nc = bass.Bass("TRN2", target_bir_lowering=False)
    t = nc.tensor
    v = nc.vector
    sc = nc.scalar
    g = nc.gpsimd
    sp = nc.sync

    xb = nc.dram_tensor("xb", [S, D], F32, kind="ExternalInput").ap()
    xq = nc.dram_tensor("xq", [TOK, D], F32, kind="ExternalInput").ap()
    Wq = nc.dram_tensor("Wq", [D, D], F32, kind="ExternalInput").ap()
    Wk = nc.dram_tensor("Wk", [D, 512], F32, kind="ExternalInput").ap()
    Wv = nc.dram_tensor("Wv", [D, 512], F32, kind="ExternalInput").ap()
    Wo = nc.dram_tensor("Wo", [D, D], F32, kind="ExternalInput").ap()
    bq = nc.dram_tensor("bq", [1, D], F32, kind="ExternalInput").ap()
    bk = nc.dram_tensor("bk", [1, 512], F32, kind="ExternalInput").ap()
    bv = nc.dram_tensor("bv", [1, 512], F32, kind="ExternalInput").ap()
    bo = nc.dram_tensor("bo", [1, D], F32, kind="ExternalInput").ap()
    rw = nc.dram_tensor("rw", [NE, D], F32, kind="ExternalInput").ap()
    w1 = nc.dram_tensor("w1", [NE, D, FF], F32, kind="ExternalInput").ap()
    w2 = nc.dram_tensor("w2", [NE, FF, D], F32, kind="ExternalInput").ap()
    w3 = nc.dram_tensor("w3", [NE, D, FF], F32, kind="ExternalInput").ap()
    y = nc.dram_tensor("y", [TOK, D], F32, kind="ExternalOutput").ap()
    g_dram = nc.dram_tensor("g_scratch", [64, 128], F32, kind="Internal").ap()

    dbg = {}
    for name, shape, dt in debug_outputs:
        dbg[name] = nc.dram_tensor(name, list(shape), dt, kind="ExternalOutput").ap()

    ctx = ExitStack()
    with ctx:
        p = P(nc, ctx)

        def dbg_store(name, src_ap):
            if name in dbg:
                p.wait("sp", "dve")
                p.wait("sp", "pe")
                p.wait("sp", "act")
                p.wait("sp", "pool")
                i = sp.dma_start(dbg[name], src_ap)
                p.dtag(i, "d_dbg")

        ones_bf = ctx.enter_context(nc.sbuf_tensor("ones_bf", [1, 128], BF16))
        ones_f = ctx.enter_context(nc.sbuf_tensor("ones_f", [1, 128], F32))
        onec_bf = ctx.enter_context(nc.sbuf_tensor("onec_bf", [128, 1], BF16))
        id_bf = ctx.enter_context(nc.sbuf_tensor("id_bf", [128, 128], BF16))
        id_f = ctx.enter_context(nc.sbuf_tensor("id_f", [128, 128], F32))
        cbias = ctx.enter_context(nc.sbuf_tensor("cbias", [128, 16], F32))
        ones128 = ctx.enter_context(nc.sbuf_tensor("ones128", [128, 128], BF16))
        p.tag(v.memset(ones128[:], 1.0), "dve")
        eps_col = ctx.enter_context(nc.sbuf_tensor("eps_col", [128, 1], F32))
        p.tag(v.memset(eps_col[:], EPS), "dve")
        p.tag(v.memset(ones_bf[:], 1.0), "dve")
        p.tag(v.memset(ones_f[:], 1.0), "dve")
        p.tag(v.memset(onec_bf[:], 1.0), "dve")
        p.tag(g.memset(id_bf[:], 0.0), "pool")
        p.selfwait("pool")
        i = g.affine_select(out=id_bf[:], in_=id_bf[:], compare_op=ALU.not_equal,
                            fill=1.0, base=0, pattern=[[-1, 128]], channel_multiplier=1)
        p.tag(i, "pool")
        p.tag(g.memset(id_f[:], 0.0), "pool")
        p.selfwait("pool")
        i = g.affine_select(out=id_f[:], in_=id_f[:], compare_op=ALU.not_equal,
                            fill=1.0, base=0, pattern=[[-1, 128]], channel_multiplier=1)
        p.tag(i, "pool")

        X = ctx.enter_context(nc.sbuf_tensor("X", [128, NDC * 1024], F32))

        rfm = ctx.enter_context(nc.sbuf_tensor("rfm", [128, NDC * NE], BF16))
        with nc.sbuf_tensor("rt_bf", [NE, D], BF16) as rt_bf, \
             nc.psum_tensor("rt_ps", [128, NE], BF16) as rt_ps:
            i = g.dma_start(rt_bf[:], rw)
            p.dtag(i, "d_rt")
            p.wait("pe", "d_rt")
            p.wait("pe", "pool")
            prev_evac = 0
            for dc in range(NDC):
                if prev_evac:
                    p.wait("pe", "dve", prev_evac)
                i = t.transpose(rt_ps[:], rt_bf[:, dc * 128:(dc + 1) * 128],
                                id_bf[0:NE, 0:NE])
                tk = p.tag(i, "pe")
                p.wait("dve", "pe", tk)
                i = v.tensor_copy(rfm[:, dc * NE:(dc + 1) * NE], rt_ps[:])
                prev_evac = p.tag(i, "dve")
        p.barrier()

        attn_ctx = ctx.enter_context(ExitStack())
        Kfm = attn_ctx.enter_context(nc.sbuf_tensor("Kfm", [128, KVH * S], BF16))
        Vsb = attn_ctx.enter_context(nc.sbuf_tensor("Vsb", [128, NT * 512], BF16))
        norms_ctx = ctx.enter_context(ExitStack())
        knmax = norms_ctx.enter_context(nc.sbuf_tensor("knmax", [128, KVH], F32, side="right"))
        kn_t = norms_ctx.enter_context(nc.sbuf_tensor("kn_t", [128, KVH], F32, side="right"))
        qn_row = norms_ctx.enter_context(nc.sbuf_tensor("qn_row", [1, H], F32, side="right"))
        qn_tmp = norms_ctx.enter_context(nc.sbuf_tensor("qn_tmp", [1, 1], F32, side="right"))
        kn_row = norms_ctx.enter_context(nc.sbuf_tensor("kn_row", [1, 512], F32, side="right"))
        knm4 = norms_ctx.enter_context(nc.sbuf_tensor("knm4", [1, KVH], F32, side="right"))
        negc = norms_ctx.enter_context(nc.sbuf_tensor("negc", [1, H], F32, side="right"))
        knT_sb = norms_ctx.enter_context(nc.sbuf_tensor("knT_sb", [KVH, 128], F32, side="right"))
        p.tag(v.memset(knmax[:], 0.0), "dve")
        p.tag(v.memset(qn_row[:], 0.0), "dve")

        with nc.sbuf_tensor("wk_sb", [128, NDC * 512], BF16) as wk_sb, \
             nc.sbuf_tensor("wv_sb", [128, NDC * 512], BF16) as wv_sb, \
             nc.sbuf_tensor("xt_bf", [128, 2 * D], BF16) as xt_bf, \
             nc.sbuf_tensor("xfm", [128, 2 * D], BF16) as xfm, \
             nc.sbuf_tensor("ktmp", [128, 2 * 512], BF16) as ktmp, \
             nc.sbuf_tensor("ksq", [128, 2 * 512], F32) as ksq, \
             nc.psum_tensor("tr_ps", [128, 3 * 1024], BF16) as tr_ps, \
             nc.psum_tensor("pk", [128, 2 * 512], F32) as pk, \
             nc.psum_tensor("pv", [128, 2 * 512], F32) as pv, \
             nc.sbuf_tensor("bk_sb", [1, 512], F32) as bk_sb, \
             nc.sbuf_tensor("bv_sb", [1, 512], F32) as bv_sb, \
             nc.sbuf_tensor("bkbc", [128, 512], F32) as bkbc, \
             nc.sbuf_tensor("bvbc", [128, 512], F32) as bvbc, \
             nc.psum_tensor("ktr", [128, 512], BF16) as ktr:
            i = sp.dma_start(bk_sb[:], bk); p.dtag(i, "d_bias")
            i = sp.dma_start(bv_sb[:], bv); p.dtag(i, "d_bias")
            bias_tok = p.cnt["d_bias"]
            p.wait("pe", "d_bias", bias_tok)
            p.wait("pe", "pool")
            i = t.matmul(pk[:, 0:512], ones_f[0:1, 0:128], bk_sb[:],
                         start=True, stop=True)
            p.tag(i, "pe")
            i = t.matmul(pv[:, 0:512], ones_f[0:1, 0:128], bv_sb[:],
                         start=True, stop=True)
            tk = p.tag(i, "pe")
            p.wait("dve", "pe", tk)
            i = v.tensor_copy(bkbc[:], pk[:, 0:512])
            p.tag(i, "dve")
            i = v.tensor_copy(bvbc[:], pv[:, 0:512])
            bbc_tok = p.tag(i, "dve")
            xdma = [0] * NT
            for tt0 in range(2):
                i = g.dma_start(xt_bf[:, tt0 * D:(tt0 + 1) * D],
                                xb[tt0 * 128:(tt0 + 1) * 128, :])
                xdma[tt0] = p.dtag(i, f"d_xb{tt0}")
            wkv_qtok = [0] * 4
            for qq in range(4):
                i = g.dma_start(
                    wk_sb[:, qq * 2048:(qq + 1) * 2048]
                    .rearrange("p (c m) -> p c m", m=512),
                    Wk[qq * 512:(qq + 1) * 512, :]
                    .rearrange("(c p) m -> p c m", p=128))
                p.dtag(i, "d_wkv")
                i = g.dma_start(
                    wv_sb[:, qq * 2048:(qq + 1) * 2048]
                    .rearrange("p (c m) -> p c m", m=512),
                    Wv[qq * 512:(qq + 1) * 512, :]
                    .rearrange("(c p) m -> p c m", p=128))
                wkv_qtok[qq] = p.dtag(i, "d_wkv")

            trdve = [0] * NT
            trevac = [0] * (2 * NT)
            trpe = [0] * NT
            mmpe = [0] * NT
            kvact = [0] * NT
            ksqact = [0] * NT
            knred = [0] * NT
            ktrpe = [0] * NT
            kfmdve = [0] * NT

            def emit_kn(tx):
                px = tx % 2
                p.selfwait("dve")
                p.wait("dve", "act", ksqact[tx])
                for kv in range(KVH):
                    i = v.tensor_reduce(kn_t[:, kv:kv + 1],
                                        ksq[:, px * 512 + kv * 128: px * 512 + (kv + 1) * 128],
                                        AX.X, ALU.add)
                    p.tag(i, "dve")
                p.selfwait("dve")
                i = v.tensor_max(knmax[:], knmax[:], kn_t[:])
                knred[tx] = p.tag(i, "dve")

            def emit_ktr(tx):
                px = tx % 2
                p.wait("pe", "dve", kvact[tx])
                if tx >= 1:
                    p.wait("pe", "dve", kfmdve[tx - 1])
                for kv in range(KVH):
                    i = t.transpose(ktr[:, kv * 128:(kv + 1) * 128],
                                    ktmp[:, px * 512 + kv * 128: px * 512 + (kv + 1) * 128],
                                    id_bf[:])
                    p.tag(i, "pe")
                ktrpe[tx] = p.cnt["pe"]
                p.wait("dve", "pe", ktrpe[tx])
                for kv in range(KVH):
                    i = v.tensor_copy(Kfm[:, kv * S + tx * 128: kv * S + (tx + 1) * 128],
                                      ktr[:, kv * 128:(kv + 1) * 128])
                    kfmdve[tx] = p.tag(i, "dve")

            def emit_trans(tx):
                par = tx % 2
                p.wait("pe", f"d_xb{par}", xdma[tx])
                for grp in range(2):
                    gidx = tx * 2 + grp
                    sl = gidx % 3
                    if gidx >= 3:
                        p.wait("pe", "dve", trevac[gidx - 3])
                    for j in range(8):
                        dc = grp * 8 + j
                        i = t.transpose(
                            tr_ps[:, sl * 1024 + j * 128: sl * 1024 + (j + 1) * 128],
                            xt_bf[:, par * D + dc * 128: par * D + (dc + 1) * 128],
                            id_bf[:])
                        p.tag(i, "pe")
                    trpe[tx] = p.cnt["pe"]
                    p.wait("dve", "pe", trpe[tx])
                    if tx >= 2:
                        p.wait("dve", "pe", mmpe[tx - 2])
                    i = v.tensor_copy(
                        xfm[:, par * D + grp * 1024: par * D + (grp + 1) * 1024],
                        tr_ps[:, sl * 1024:(sl + 1) * 1024])
                    trevac[gidx] = p.tag(i, "dve")

            emit_trans(0)
            for tt in range(NT):
                par = tt % 2
                if tt + 2 < NT:
                    p.wait("pool", "pe", trpe[tt])
                    i = g.dma_start(xt_bf[:, par * D:(par + 1) * D],
                                    xb[(tt + 2) * 128:(tt + 3) * 128, :])
                    xdma[tt + 2] = p.dtag(i, f"d_xb{par}")
                p.wait("pe", "dve", trevac[2 * tt])
                if tt == 0:
                    p.wait("pe", "dve", bbc_tok)
                    p.wait("pe", "d_wkv", wkv_qtok[1])
                if tt >= 2:
                    p.wait("pe", "dve", kvact[tt - 2])
                for dc in range(8):
                    t.matmul(pk[:, par * 512:(par + 1) * 512],
                             xfm[:, par * D + dc * 128: par * D + (dc + 1) * 128],
                             wk_sb[:, dc * 512:(dc + 1) * 512],
                             start=(dc == 0), stop=False)
                    t.matmul(pv[:, par * 512:(par + 1) * 512],
                             xfm[:, par * D + dc * 128: par * D + (dc + 1) * 128],
                             wv_sb[:, dc * 512:(dc + 1) * 512],
                             start=(dc == 0), stop=False)
                if tt + 1 < NT:
                    emit_trans(tt + 1)
                if tt >= 1:
                    emit_ktr(tt - 1)
                p.wait("pe", "dve", trevac[2 * tt + 1])
                if tt == 0:
                    p.wait("pe", "d_wkv", wkv_qtok[3])
                for dc in range(8, NDC):
                    t.matmul(pk[:, par * 512:(par + 1) * 512],
                             xfm[:, par * D + dc * 128: par * D + (dc + 1) * 128],
                             wk_sb[:, dc * 512:(dc + 1) * 512],
                             start=False, stop=(dc == NDC - 1))
                    i = t.matmul(pv[:, par * 512:(par + 1) * 512],
                                 xfm[:, par * D + dc * 128: par * D + (dc + 1) * 128],
                                 wv_sb[:, dc * 512:(dc + 1) * 512],
                                 start=False, stop=(dc == NDC - 1))
                mmpe[tt] = p.tag(i, "pe")

                p.wait("dve", "pe", mmpe[tt])
                if tt >= 2:
                    p.wait("dve", "pe", ktrpe[tt - 2])
                i = v.tensor_tensor(ktmp[:, par * 512:(par + 1) * 512],
                                    pk[:, par * 512:(par + 1) * 512],
                                    bkbc[:], ALU.add)
                p.tag(i, "dve")
                i = v.tensor_tensor(Vsb[:, tt * 512:(tt + 1) * 512],
                                    pv[:, par * 512:(par + 1) * 512],
                                    bvbc[:], ALU.add)
                kvact[tt] = p.tag(i, "dve")
                p.wait("act", "dve", kvact[tt])
                if tt >= 2:
                    p.wait("act", "dve", knred[tt - 2])
                i = sc.activation(ksq[:, par * 512:(par + 1) * 512],
                                  ktmp[:, par * 512:(par + 1) * 512], AF.Square)
                ksqact[tt] = p.tag(i, "act")
                if tt >= 1:
                    emit_kn(tt - 1)
            emit_kn(NT - 1)
            emit_ktr(NT - 1)
        p.barrier()
        Qfm = attn_ctx.enter_context(nc.sbuf_tensor("Qfm", [128, H * 1024], BF16))

        with nc.sbuf_tensor("xot", [128, 3 * 1024], F32) as xot, \
             nc.sbuf_tensor("wq_sb", [128, 3 * 2048], BF16) as wq_sb, \
             nc.sbuf_tensor("xh_bf", [128, NDC * 512], BF16) as xh_bf, \
             nc.sbuf_tensor("bqv", [16, 128], BF16) as bqv, \
             nc.sbuf_tensor("bq_col", [128, 16], F32) as bq_col, \
             nc.sbuf_tensor("qsq", [128, 2 * 512], BF16) as qsq, \
             nc.psum_tensor("xo_ps", [128, 2 * 512], F32) as xo_ps, \
             nc.psum_tensor("q_ps", [128, 3 * 512], F32) as q_ps, \
             nc.psum_tensor("qn2", [128, 512], F32) as qn2:
            wq_tok = [0] * NDC

            def wq_load(m):
                i = g.dma_start(
                    wq_sb[:, (m % 3) * 2048:((m % 3) + 1) * 2048]
                    .rearrange("p (c n) -> p c n", n=128),
                    Wq[:, m * 128:(m + 1) * 128].rearrange("(c p) n -> p c n", p=128))
                wq_tok[m] = p.dtag(i, f"d_wq{m % 3}")

            for mq in range(3):
                wq_load(mq)
            i = g.dma_start(bqv[:], bq.rearrange("o (m q) -> (o m) q", q=128))
            bq_tok2 = p.dtag(i, "d_bqv")
            xq_dma = [0] * (2 * NOT)
            xo_pe = [0] * (2 * NOT)
            xo_dve = {}
            for ot in range(NOT):
                for hh in range(2):
                    hidx = ot * 2 + hh
                    sl = hidx % 3
                    if hidx >= 3:
                        p.wait("sp", "pe", xo_pe[hidx - 3])
                    i = sp.dma_start(xot[:, sl * 1024:(sl + 1) * 1024],
                                     xq[ot * 128:(ot + 1) * 128,
                                        hh * 1024:(hh + 1) * 1024])
                    xq_dma[hidx] = p.dtag(i, f"d_xq{sl}")
                    p.wait("pe", f"d_xq{sl}", xq_dma[hidx])
                    for grp in range(2):
                        gid = hidx * 2 + grp
                        gp = gid % 2
                        if gid >= 2:
                            p.wait("pe", "dve", xo_dve[gid - 2])
                        for j in range(4):
                            i = t.transpose(
                                xo_ps[:, gp * 512 + j * 128: gp * 512 + (j + 1) * 128],
                                xot[:, sl * 1024 + (grp * 4 + j) * 128:
                                    sl * 1024 + (grp * 4 + j + 1) * 128],
                                id_f[:])
                            p.tag(i, "pe")
                        xo_pe[hidx] = p.cnt["pe"]
                        p.wait("dve", "pe", xo_pe[hidx])
                        for j in range(4):
                            dc = hh * 8 + grp * 4 + j
                            i = v.tensor_copy(
                                X[:, dc * 1024 + ot * 128: dc * 1024 + (ot + 1) * 128],
                                xo_ps[:, gp * 512 + j * 128: gp * 512 + (j + 1) * 128])
                            xo_dve[gid] = p.tag(i, "dve")
            x_done_dve = p.cnt["dve"]
            p.wait("pe", "d_bqv", bq_tok2)
            with nc.psum_tensor("bqc_ps", [128, 16], BF16) as bqc_ps:
                i = t.transpose(bqc_ps[:], bqv[:], id_bf[0:16, 0:16])
                tk = p.tag(i, "pe")
                p.wait("dve", "pe", tk)
                i = v.tensor_copy(bq_col[:], bqc_ps[:])
                bqcol_tok = p.tag(i, "dve")
            q_evac = {}
            qn_red = {}
            qsq_tok = {}
            slot_i = 0
            xh_cast = {}

            def emit_qn(s, mm, spar_s):
                p.wait("pe", "act", qsq_tok[s])
                if s >= 1:
                    p.wait("pe", "dve", qn_red[s - 1])
                i = t.matmul(qn2[:], ones128[:],
                             qsq[:, spar_s * 512:(spar_s + 1) * 512],
                             start=True, stop=True)
                qn_pe = p.tag(i, "pe")
                p.wait("dve", "pe", qn_pe)
                i = v.tensor_reduce(qn_tmp[:], qn2[0:1, :], AX.X, ALU.max)
                p.tag(i, "dve")
                p.selfwait("dve")
                i = v.tensor_max(qn_row[0:1, mm:mm + 1], qn_row[0:1, mm:mm + 1],
                                 qn_tmp[:])
                qn_red[s] = p.tag(i, "dve")

            for half in range(2):
                if half == 0:
                    p.wait("act", "pe")
                    for dc in range(NDC):
                        g_need = (3 * 2 + dc // 8) * 2 + (dc % 8) // 4
                        p.wait("act", "dve", xo_dve[g_need])
                        i = sc.copy(
                            xh_bf[:, dc * 512:(dc + 1) * 512],
                            X[:, dc * 1024 + half * 512: dc * 1024 + (half + 1) * 512])
                        xh_cast[dc] = p.tag(i, "act")
                for mi in range(NDC):
                    m = mi if half == 0 else NDC - 1 - mi
                    p.wait("pe", f"d_wq{m % 3}", wq_tok[m])
                    slot = slot_i % 3
                    spar = slot_i % 2
                    if slot_i >= 3:
                        p.wait("pe", "dve", q_evac[slot_i - 3])
                    for dc in range(NDC):
                        if mi == 0:
                            p.wait("pe", "act", xh_cast[dc])
                        if dc == 8 and slot_i >= 1:
                            emit_qn(slot_i - 1, *qn_args)
                        i = t.matmul(q_ps[:, slot * 512:(slot + 1) * 512],
                                     wq_sb[:, (m % 3) * 2048 + dc * 128:
                                           (m % 3) * 2048 + (dc + 1) * 128],
                                     xh_bf[:, dc * 512:(dc + 1) * 512],
                                     start=(dc == 0), stop=(dc == NDC - 1))
                        if half == 0 and m == NDC - 1 and dc < NDC - 1:
                            tk = p.tag(i, "pe")
                            p.wait("act", "pe", tk)
                            i2 = sc.copy(
                                xh_bf[:, dc * 512:(dc + 1) * 512],
                                X[:, dc * 1024 + 512: dc * 1024 + 1024])
                            xh_cast[dc] = p.tag(i2, "act")
                    q_pe = p.tag(i, "pe")
                    if half == 0 and m == NDC - 1:
                        dc = NDC - 1
                        p.wait("act", "pe", q_pe)
                        i2 = sc.copy(
                            xh_bf[:, dc * 512:(dc + 1) * 512],
                            X[:, dc * 1024 + 512: dc * 1024 + 1024])
                        xh_cast[dc] = p.tag(i2, "act")
                    p.wait("dve", "pe", q_pe)
                    p.wait("dve", "dve", bqcol_tok)
                    i = v.tensor_scalar(
                        Qfm[:, m * 1024 + half * 512: m * 1024 + (half + 1) * 512],
                        q_ps[:, slot * 512:(slot + 1) * 512],
                        bq_col[:, m:m + 1], None, ALU.add)
                    q_evac[slot_i] = p.tag(i, "dve")
                    p.wait("act", "dve", q_evac[slot_i])
                    if slot_i >= 2:
                        p.wait("act", "dve", qn_red[slot_i - 2])
                    i = sc.activation(
                        qsq[:, spar * 512:(spar + 1) * 512],
                        Qfm[:, m * 1024 + half * 512: m * 1024 + (half + 1) * 512],
                        AF.Square)
                    qsq_tok[slot_i] = p.tag(i, "act")
                    qn_args = (m, spar)
                    slot_i += 1
                    nxt = m + 3 if half == 0 else m - 3
                    if 0 <= nxt < NDC:
                        p.wait("pool", "pe", q_pe)
                        wq_load(nxt)
            emit_qn(slot_i - 1, *qn_args)
        p.barrier(exclude=("d_coll",))
        dbg_store("dQ", Qfm[:])
        dbg_store("dK", Kfm[:])
        dbg_store("dV", Vsb[:])

        with nc.psum_tensor("knT_ps", [KVH, 128], F32) as knT_ps, \
             nc.psum_tensor("cb_ps", [128, 16], F32) as cb_ps:
            i = t.transpose(knT_ps[:], knmax[:], id_f[:])
            tk = p.tag(i, "pe")
            p.wait("dve", "pe", tk)
            i = v.tensor_copy(knT_sb[:], knT_ps[:])
            tk = p.tag(i, "dve")
            p.wait("sp", "dve", tk)
            i = sp.dma_start(kn_row[0:1, :], knT_sb[:, :])
            tk = p.dtag(i, "d_kn")
            p.wait("dve", "d_kn", tk)
            for kv in range(KVH):
                i = v.tensor_reduce(knm4[0:1, kv:kv + 1],
                                    kn_row[0:1, kv * 128:(kv + 1) * 128], AX.X, ALU.max)
                p.tag(i, "dve")
            p.selfwait("dve")
            for kv in range(KVH):
                i = v.tensor_scalar(negc[0:1, kv * 4:(kv + 1) * 4],
                                    qn_row[0:1, kv * 4:(kv + 1) * 4],
                                    knm4[0:1, kv:kv + 1], KAPPA,
                                    ALU.add, ALU.mult)
                p.tag(i, "dve")
            p.wait("pe", "dve")
            i = t.matmul(cb_ps[:], ones_f[0:1, 0:128], negc[0:1, :],
                         start=True, stop=True)
            tk = p.tag(i, "pe")
            p.wait("dve", "pe", tk)
            i = v.tensor_copy(cbias[:], cb_ps[:])
            cbias_tok = p.tag(i, "dve")
        p.barrier()
        norms_ctx.close()

        ln_id = [0]
        last_cast = [0]

        class LN:

            def __init__(self):
                ln = ln_id[0]
                ln_id[0] += 1
                self.sc = ExitStack()
                E = self.sc.enter_context
                self.tbf = E(nc.sbuf_tensor(f"tbf{ln}", [128, 2 * 1024], BF16))
                self.tsq = E(nc.sbuf_tensor(f"tsq{ln}", [128, 2 * 1024], BF16))
                self.bcm_sb = E(nc.sbuf_tensor(f"bcm_sb{ln}", [128, 1024], F32))
                self.bci_sb = E(nc.sbuf_tensor(f"bci_sb{ln}", [128, 1024], F32))
                self.psc = ExitStack()
                self.s1 = self.psc.enter_context(nc.psum_tensor(f"s1{ln}", [128, 2 * 512], F32))
                self.s2 = self.psc.enter_context(nc.psum_tensor(f"s2{ln}", [128, 2 * 512], F32))
                self.mm_tok = {}
                self.sq_tok = {}
                self.cast_tok = {}
                self.x_tok = {}

            def chunk(self, dc, xtok, defer_stats=False):
                self.x_tok[dc] = xtok
                par = dc % 2
                p.wait("act", "dve", xtok)
                if dc >= 2:
                    p.wait("act", "pe", self.mm_tok[dc - 2])
                i = sc.copy(self.tbf[:, par * 1024:(par + 1) * 1024],
                            X[:, dc * 1024:(dc + 1) * 1024])
                self.cast_tok[dc] = p.tag(i, "act")
                p.selfwait("act")
                i = sc.activation(self.tsq[:, par * 1024:(par + 1) * 1024],
                                  self.tbf[:, par * 1024:(par + 1) * 1024], AF.Square)
                self.sq_tok[dc] = p.tag(i, "act")
                if not defer_stats:
                    self.chunk_stats(dc)

            def chunk_stats(self, dc):
                par = dc % 2
                p.wait("pe", "act", self.sq_tok[dc])
                for hf in range(2):
                    t.matmul(self.s1[:, hf * 512:(hf + 1) * 512], ones128[:],
                             self.tbf[:, par * 1024 + hf * 512: par * 1024 + (hf + 1) * 512],
                             start=(dc == 0), stop=(dc == NDC - 1))
                    i = t.matmul(self.s2[:, hf * 512:(hf + 1) * 512], ones128[:],
                                 self.tsq[:, par * 1024 + hf * 512: par * 1024 + (hf + 1) * 512],
                                 start=(dc == 0), stop=(dc == NDC - 1))
                self.mm_tok[dc] = p.tag(i, "pe")

            def finalize(self, cast_to=None, apply=True):
                self.norm_tok = {}
                self.out_cast = {}
                bcm_sb, bci_sb = self.bcm_sb, self.bci_sb
                p.wait("dve", "pe", self.mm_tok[NDC - 1])
                i = v.tensor_scalar(bcm_sb[:], self.s1[:], INV_D, None, ALU.mult)
                tk = p.tag(i, "dve")
                p.wait("act", "dve", tk)
                i = sc.activation(bci_sb[:], bcm_sb[:], AF.Square)
                tk = p.tag(i, "act")
                p.wait("dve", "act", tk)
                i = v.scalar_tensor_tensor(bci_sb[:], self.s2[:], INV_D, bci_sb[:],
                                           ALU.mult, ALU.subtract)
                tk = p.tag(i, "dve")
                self.psc.close()
                p.wait("act", "dve", tk)
                i = sc.activation(bci_sb[:], bci_sb[:], AF.Ln, bias=eps_col[:])
                p.tag(i, "act")
                p.selfwait("act")
                i = sc.activation(bci_sb[:], bci_sb[:], AF.Exp, scale=-0.5)
                lntok = p.tag(i, "act")
                if not apply:
                    E = lambda t_: ctx.enter_context(t_)
                    self.m_col = E(nc.sbuf_tensor(f"m_col{ln_id[0]}", [128, NOT], F32, side="right"))
                    self.i_col = E(nc.sbuf_tensor(f"i_col{ln_id[0]}", [128, NOT], F32, side="right"))
                    mv8 = E(nc.sbuf_tensor(f"mv8{ln_id[0]}", [NOT, 128], F32, side="right"))
                    iv8 = E(nc.sbuf_tensor(f"iv8{ln_id[0]}", [NOT, 128], F32, side="right"))
                    p.wait("sp", "act", lntok)
                    p.wait("sp", "dve")
                    i = sp.dma_start(mv8[:], bcm_sb[0:1, :])
                    p.dtag(i, "d_lnc")
                    i = sp.dma_start(iv8[:], bci_sb[0:1, :])
                    tk = p.dtag(i, "d_lnc")
                    p.wait("pe", "d_lnc", tk)
                    with nc.psum_tensor(f"mc_ps{ln_id[0]}", [128, NOT], F32) as mc_ps, \
                         nc.psum_tensor(f"ic_ps{ln_id[0]}", [128, NOT], F32) as ic_ps:
                        i = t.transpose(mc_ps[:], mv8[:], id_f[0:NOT, 0:NOT])
                        p.tag(i, "pe")
                        i = t.transpose(ic_ps[:], iv8[:], id_f[0:NOT, 0:NOT])
                        tk = p.tag(i, "pe")
                        p.wait("dve", "pe", tk)
                        i = v.tensor_copy(self.m_col[:], mc_ps[:])
                        p.tag(i, "dve")
                        i = v.tensor_copy(self.i_col[:], ic_ps[:])
                        self.col_tok = p.tag(i, "dve")
                    self.sc.close()
                    return
                p.wait("dve", "act", lntok)
                for dc in range(NDC):
                    i = v.tensor_tensor(X[:, dc * 1024:(dc + 1) * 1024],
                                        X[:, dc * 1024:(dc + 1) * 1024],
                                        bcm_sb[:], ALU.subtract)
                    p.tag(i, "dve")
                    i = v.tensor_tensor(X[:, dc * 1024:(dc + 1) * 1024],
                                        X[:, dc * 1024:(dc + 1) * 1024],
                                        bci_sb[:], ALU.mult)
                    xtok = self.norm_tok[dc] = p.tag(i, "dve")
                    if cast_to is not None:
                        p.wait("act", "dve", xtok)
                        i = sc.copy(cast_to[:, dc * 1024:(dc + 1) * 1024],
                                    X[:, dc * 1024:(dc + 1) * 1024])
                        last_cast[0] = self.out_cast[dc] = p.tag(i, "act")
                self.sc.close()
                p.barrier(engines=("act", "dve", "pool", "sp"))

        octx = ctx.enter_context(ExitStack())
        Afm = octx.enter_context(nc.sbuf_tensor("Afm", [128, H * 1024], BF16, side="right"))
        wo_slots = [None, None]
        wo_tok = [0] * NDC

        def wo_load(m):
            i = g.dma_start(
                wo_slots[m % 2][:].rearrange("p (c n) -> p c n", n=128),
                Wo[:, m * 128:(m + 1) * 128].rearrange("(c p) n -> p c n", p=128))
            wo_tok[m] = p.dtag(i, f"d_wo{m % 2}")
        units = [(h, qb) for h in range(H) for qb in range(2)]
        NU = len(units)
        TOTG = NU * NG
        NJ = NG // 2
        with nc.sbuf_tensor("pt", [128, 3 * 1024], BF16) as pt, \
             nc.sbuf_tensor("padd", [128, 2 * 512], BF16) as padd, \
             nc.sbuf_tensor("padd2", [128, 4 * 512], BF16) as padd2, \
             nc.sbuf_tensor("rden", [128, 512], BF16) as rden, \
             nc.psum_tensor("st_ps", [128, 2 * 1024], F32) as st_ps, \
             nc.psum_tensor("a_ps", [128, 2 * 512], F32) as a_ps, \
             nc.psum_tensor("den_ps", [128, 2 * 512], F32) as den_ps:
            p.wait("act", "dve", cbias_tok)
            exp_tok = {}
            pv_tok = {}
            padd_tok = {}
            padd2_tok = {}
            sum_pe = {}
            summ_tok = {}
            recip_tok = {}
            evac_tok = {}

            def emit_scores(gg):
                u, gi = divmod(gg, NG)
                h, qb = units[u]
                kv = h // 4
                qcol = h * 1024 + qb * 512
                sl2 = gg % 2
                if gg >= 2:
                    p.wait("pe", "act", exp_tok[gg - 2])
                for c2 in range(2):
                    kc = gi * 2 + c2
                    i = t.matmul(
                        st_ps[:, sl2 * 1024 + c2 * 512: sl2 * 1024 + (c2 + 1) * 512],
                        Kfm[:, kv * S + kc * 128: kv * S + (kc + 1) * 128],
                        Qfm[:, qcol:qcol + 512], start=True, stop=True)
                st_pe = p.tag(i, "pe")
                p.wait("act", "pe", st_pe)
                if gg >= 3:
                    p.wait("act", "pe", pv_tok[gg - 3])
                    p.wait("act", "dve", padd_tok[gg - 3])
                i = sc.activation(pt[:, (gg % 3) * 1024:(gg % 3 + 1) * 1024],
                                  st_ps[:, sl2 * 1024:(sl2 + 1) * 1024],
                                  AF.Exp, bias=cbias[:, h:h + 1], scale=SCALE)
                exp_tok[gg] = p.tag(i, "act")
                p.wait("dve", "act", exp_tok[gg])
                with nc.allow_low_precision(reason="bf16 softmax normalizer, within bar"):
                    i = v.tensor_tensor(padd[:, (gg % 2) * 512:(gg % 2 + 1) * 512],
                                        pt[:, (gg % 3) * 1024:(gg % 3) * 1024 + 512],
                                        pt[:, (gg % 3) * 1024 + 512:(gg % 3 + 1) * 1024],
                                        ALU.add)
                padd_tok[gg] = p.tag(i, "dve")
                if gi % 2 == 1:
                    jg = gg // 2
                    if jg >= 4:
                        p.wait("dve", "pe", sum_pe[jg - 4])
                    with nc.allow_low_precision(reason="bf16 softmax normalizer, within bar"):
                        i = v.tensor_tensor(padd2[:, (jg % 4) * 512:(jg % 4 + 1) * 512],
                                            padd[:, 0:512], padd[:, 512:1024],
                                            ALU.add)
                    padd2_tok[jg] = p.tag(i, "dve")

            def emit_pv(gg):
                u, gi = divmod(gg, NG)
                h, qb = units[u]
                kv = h // 4
                p.wait("pe", "act", exp_tok[gg])
                if gi == 0 and u >= 2:
                    p.wait("pe", "dve", evac_tok[u - 2])
                for c2 in range(2):
                    kc = gi * 2 + c2
                    i = t.matmul(
                        a_ps[:, (u % 2) * 512:(u % 2 + 1) * 512],
                        Vsb[:, kc * 512 + kv * 128: kc * 512 + (kv + 1) * 128],
                        pt[:, (gg % 3) * 1024 + c2 * 512: (gg % 3) * 1024 + (c2 + 1) * 512],
                        start=(kc == 0), stop=(kc == NKC - 1))
                pv_tok[gg] = p.tag(i, "pe")

            def emit_sum(jg):
                u, j = divmod(jg, NJ)
                p.wait("pe", "dve", padd2_tok[jg])
                if j == 0 and u >= 2:
                    p.wait("pe", "dve", recip_tok[u - 2])
                i = t.matmul(den_ps[:, (u % 2) * 512:(u % 2 + 1) * 512],
                             ones128[:], padd2[:, (jg % 4) * 512:(jg % 4 + 1) * 512],
                             start=(j == 0), stop=(j == NJ - 1))
                sum_pe[jg] = p.tag(i, "pe")
                if j == NJ - 1:
                    summ_tok[u] = sum_pe[jg]

            def emit_den_evac(u):
                h, qb = units[u]
                qcol = h * 1024 + qb * 512
                p.wait("dve", "pe", summ_tok[u])
                p.wait("dve", "pe", pv_tok[u * NG + NG - 1])
                with nc.allow_low_precision(reason="bf16 softmax normalizer, within bar"):
                    i = v.reciprocal(rden[:], den_ps[:, (u % 2) * 512:(u % 2 + 1) * 512])
                recip_tok[u] = p.tag(i, "dve")
                i = v.tensor_tensor(Afm[:, qcol:qcol + 512],
                                    a_ps[:, (u % 2) * 512:(u % 2 + 1) * 512],
                                    rden[:], ALU.mult)
                evac_tok[u] = p.tag(i, "dve")

            for gg in range(TOTG):
                u, gi = divmod(gg, NG)
                if gg >= 9 and gg % 2 == 1:
                    emit_sum((gg - 9) // 2)
                emit_scores(gg)
                if gg >= 2:
                    emit_pv(gg - 2)
                if gi == 11 and u >= 1:
                    emit_den_evac(u - 1)
            emit_pv(TOTG - 2)
            emit_pv(TOTG - 1)
            for jg in range(NU * NJ - 4, NU * NJ):
                emit_sum(jg)
            emit_den_evac(NU - 1)
        p.barrier()
        dbg_store("dA", Afm[:])

        attn_ctx.close()
        mctx = ctx.enter_context(ExitStack())
        xnew = mctx.enter_context(nc.sbuf_tensor("xnew", [128, NDC * 1024], BF16))

        wo_slots[0] = octx.enter_context(
            nc.sbuf_tensor("wo_sb0", [128, NDC * 128], BF16, side="right"))
        wo_slots[1] = octx.enter_context(
            nc.sbuf_tensor("wo_sb1", [128, NDC * 128], BF16, side="right"))
        wo_load(0)
        wo_load(1)
        with nc.sbuf_tensor("bo_sb", [1, D], F32) as bo_sb, \
             nc.sbuf_tensor("bov", [16, 128], F32) as bov, \
             nc.sbuf_tensor("bo_col", [128, 16], F32) as bo_col, \
             nc.psum_tensor("o_ps", [128, 2 * 1024], F32) as o_ps:
            i = sp.dma_start(bo_sb[:], bo)
            bo_tok = p.dtag(i, "d_bo")
            p.wait("sp", "d_bo", bo_tok)
            i = sp.dma_start(bov[:], bo_sb[:])
            bo_tok2 = p.dtag(i, "d_bo")
            p.wait("pe", "d_bo", bo_tok2)
            with nc.psum_tensor("boc_ps", [128, 16], F32) as boc_ps:
                i = t.transpose(boc_ps[:], bov[:], id_f[0:16, 0:16])
                tk = p.tag(i, "pe")
                p.wait("dve", "pe", tk)
                i = v.tensor_copy(bo_col[:], boc_ps[:])
                bocol_tok = p.tag(i, "dve")
            ln1 = LN()
            oadd = {}
            for m in range(NDC):
                p.wait("pe", f"d_wo{m % 2}", wo_tok[m])
                if m >= 2:
                    p.wait("pe", "dve", oadd[m - 2])
                par = m % 2
                for fc in range(NDC):
                    if fc == 8 and m >= 1:
                        ln1.chunk_stats(m - 1)
                    for hf in range(2):
                        i = t.matmul(o_ps[:, par * 1024 + hf * 512: par * 1024 + (hf + 1) * 512],
                                     wo_slots[m % 2][:, fc * 128:(fc + 1) * 128],
                                     Afm[:, fc * 1024 + hf * 512: fc * 1024 + (hf + 1) * 512],
                                     start=(fc == 0), stop=(fc == NDC - 1))
                o_pe = p.tag(i, "pe")
                p.wait("dve", "pe", o_pe)
                p.wait("dve", "dve", bocol_tok)
                i = v.scalar_tensor_tensor(X[:, m * 1024:(m + 1) * 1024],
                                           o_ps[:, par * 1024:(par + 1) * 1024],
                                           bo_col[:, m:m + 1],
                                           X[:, m * 1024:(m + 1) * 1024],
                                           ALU.add, ALU.add)
                oadd[m] = p.tag(i, "dve")
                ln1.chunk(m, oadd[m], defer_stats=True)
                if m + 2 < NDC:
                    p.wait("pool", "pe", o_pe)
                    wo_load(m + 2)
            ln1.chunk_stats(NDC - 1)
            dbg_store("dR1", X[:])
            ln1.finalize(cast_to=xnew)
        octx.close()

        with nc.sbuf_tensor("h_all", [128, NE * NFC * 1024], BF16) as h_all:
            xcast = last_cast[0]
            dbg_store("dLN1", X[:])
            w2ctx = ExitStack()
            w2m = w2ctx.enter_context(nc.sbuf_tensor("w2m", [128, 2 * 4096], BF16))
            NPC = 2
            w13s = ExitStack()
            w1_sb = w13s.enter_context(nc.sbuf_tensor("w1_sb", [128, NPC * NDC * 128], BF16))
            w3_sb = w13s.enter_context(nc.sbuf_tensor("w3_sb", [128, NPC * NDC * 128], BF16))
            ge_row = w13s.enter_context(nc.sbuf_tensor("ge_row", [1, 2 * 1024], BF16))
            bcg_sb = w13s.enter_context(nc.sbuf_tensor("bcg_sb", [128, 2 * 1024], BF16))
            sil = w13s.enter_context(nc.sbuf_tensor("sil", [128, 2 * 512], BF16))
            t1 = w13s.enter_context(nc.sbuf_tensor("t1", [128, 2 * 512], BF16))
            PSZ = NDC * 128
            w13_tok = [0] * (NE * NFC)

            def w13_load(pi):
                e, fc = pi // NFC, pi % NFC
                sl = pi % NPC
                i = g.dma_start(
                    w1_sb[:, sl * PSZ:(sl + 1) * PSZ].rearrange("p (c m) -> p c m", m=128),
                    w1[e][:, fc * 128:(fc + 1) * 128].rearrange("(c p) m -> p c m", p=128))
                p.dtag(i, f"d_w13{sl}")
                i = g.dma_start(
                    w3_sb[:, sl * PSZ:(sl + 1) * PSZ].rearrange("p (c m) -> p c m", m=128),
                    w3[e][:, fc * 128:(fc + 1) * 128].rearrange("(c p) m -> p c m", p=128))
                w13_tok[pi] = p.dtag(i, f"d_w13{sl}")

            for pi in range(NPC):
                w13_load(pi)
            with nc.sbuf_tensor("r_all", [128, 64], F32) as r_all, \
                 nc.sbuf_tensor("m1a", [128, NE], F32) as m1a, \
                 nc.sbuf_tensor("m2a", [128, NE], F32) as m2a, \
                 nc.sbuf_tensor("mk1", [128, NE * NE], F32) as mk1, \
                 nc.sbuf_tensor("mk2", [128, NE * NE], F32) as mk2, \
                 nc.sbuf_tensor("sigin", [128, NE], F32) as sigin, \
                 nc.sbuf_tensor("sig", [128, NE], F32) as sig, \
                 nc.sbuf_tensor("gT_sb", [64, 128], F32) as gT_sb, \
                 nc.psum_tensor("r_ps", [128, 64], F32) as r_ps, \
                 nc.psum_tensor("gT_ps", [64, 128], F32) as gT_ps, \
                 nc.psum_tensor("x1_ps", [128, 2 * 512], F32) as x1_ps, \
                 nc.psum_tensor("x3_ps", [128, 2 * 512], F32) as x3_ps, \
                 nc.psum_tensor("bcg_ps", [128, 2 * 512], F32) as bcg_ps:
                r2 = r_all
                gts = mk1
                for dc in range(NDC):
                    p.wait("pe", "act", ln1.out_cast[dc])
                    for qt in range(8):
                        i = t.matmul(r_ps[:, qt * NE:(qt + 1) * NE],
                                     xnew[:, dc * 1024 + qt * 128: dc * 1024 + (qt + 1) * 128],
                                     rfm[:, dc * NE:(dc + 1) * NE],
                                     start=(dc == 0 and qt == 0),
                                     stop=(dc == NDC - 1 and qt == 7))
                rtok = p.tag(i, "pe")
                p.wait("dve", "pe", rtok)
                i = v.tensor_copy(r_all[:], r_ps[:])
                p.tag(i, "dve")
                p.selfwait("dve")
                for qt in range(8):
                    i = v.tensor_reduce(m1a[:, qt:qt + 1],
                                        r_all[:, qt * NE:(qt + 1) * NE], AX.X, ALU.max)
                    p.tag(i, "dve")
                p.selfwait("dve")
                for qt in range(8):
                    i = v.tensor_scalar(mk1[:, qt * NE:(qt + 1) * NE],
                                        r_all[:, qt * NE:(qt + 1) * NE],
                                        m1a[:, qt:qt + 1], None, ALU.is_ge)
                    p.tag(i, "dve")
                p.selfwait("dve")
                for qt in range(8):
                    i = v.scalar_tensor_tensor(r2[:, qt * NE:(qt + 1) * NE],
                                               mk1[:, qt * NE:(qt + 1) * NE], -1e30,
                                               r_all[:, qt * NE:(qt + 1) * NE],
                                               ALU.mult, ALU.add)
                    p.tag(i, "dve")
                p.selfwait("dve")
                for qt in range(8):
                    i = v.tensor_reduce(m2a[:, qt:qt + 1],
                                        r2[:, qt * NE:(qt + 1) * NE], AX.X, ALU.max)
                    p.tag(i, "dve")
                p.selfwait("dve")
                for qt in range(8):
                    i = v.tensor_scalar(mk2[:, qt * NE:(qt + 1) * NE],
                                        r2[:, qt * NE:(qt + 1) * NE],
                                        m2a[:, qt:qt + 1], None, ALU.is_ge)
                    p.tag(i, "dve")
                p.selfwait("dve")
                i = v.tensor_tensor(sigin[:], m1a[:], m2a[:], ALU.subtract)
                tk = p.tag(i, "dve")
                p.wait("act", "dve", tk)
                i = sc.activation(sig[:], sigin[:], AF.Sigmoid)
                sig_tok = p.tag(i, "act")
                p.selfwait("dve")
                i = v.tensor_tensor(mk1[:], mk1[:], mk2[:], ALU.subtract)
                p.tag(i, "dve")
                p.wait("dve", "act", sig_tok)
                p.selfwait("dve")
                gts_tok = 0
                for qt in range(8):
                    i = v.scalar_tensor_tensor(gts[:, qt * NE:(qt + 1) * NE],
                                               mk1[:, qt * NE:(qt + 1) * NE],
                                               sig[:, qt:qt + 1],
                                               mk2[:, qt * NE:(qt + 1) * NE],
                                               ALU.mult, ALU.add)
                    gts_tok = p.tag(i, "dve")
                gst_tok = 0

                def gates_finish():
                    nonlocal gst_tok
                    p.wait("pe", "dve", gts_tok)
                    i = t.transpose(gT_ps[:], gts[:], id_f[:])
                    tk = p.tag(i, "pe")
                    p.wait("dve", "pe", tk)
                    i = v.tensor_copy(gT_sb[:], gT_ps[:])
                    tk = p.tag(i, "dve")
                    p.wait("sp", "dve", tk)
                    i = sp.dma_start(g_dram, gT_sb[:])
                    gst_tok = p.dtag(i, "d_gst")

                h_done = {}
                bcg_done = [0, 0]
                ge_dma = [0] * NE
                ge_read = [0] * NE
                uidx = 0

                def ge_load(e):
                    ep = e % 2
                    p.wait("pool", "d_gst", gst_tok)
                    if e >= 2:
                        p.wait("pool", "pe", ge_read[e - 2])
                    i = g.dma_start(ge_row[0:1, ep * 1024:(ep + 1) * 1024],
                                    g_dram.rearrange("(q e) n -> e q n", e=NE)[e])
                    ge_dma[e] = p.dtag(i, f"d_ge{ep}")

                bcg_war = [0, 0]

                def bcg_make(e):
                    ep = e % 2
                    p.wait("pe", f"d_ge{ep}", ge_dma[e])
                    for th in range(2):
                        p.wait("pe", "dve", bcg_war[th])
                        i = t.matmul(bcg_ps[:, th * 512:(th + 1) * 512],
                                     ones_bf[0:1, 0:128],
                                     ge_row[0:1, ep * 1024 + th * 512: ep * 1024 + (th + 1) * 512],
                                     start=True, stop=True)
                        tk = ge_read[e] = p.tag(i, "pe")
                        p.wait("dve", "pe", tk)
                        i = v.tensor_copy(
                            bcg_sb[:, ep * 1024 + th * 512: ep * 1024 + (th + 1) * 512],
                            bcg_ps[:, th * 512:(th + 1) * 512])
                        bcg_war[th] = p.tag(i, "dve")
                    bcg_done[ep] = bcg_war[1]

                def emit_t1h(ui, up, pi_, th_, sil_tok, x13_tok):
                    ep_ = (pi_ // NFC) % 2
                    p.wait("dve", "act", sil_tok)
                    p.wait("dve", "pe", x13_tok)
                    i = v.tensor_tensor(t1[:, up * 512:(up + 1) * 512],
                                        sil[:, up * 512:(up + 1) * 512],
                                        x3_ps[:, up * 512:(up + 1) * 512], ALU.mult)
                    p.tag(i, "dve")
                    p.selfwait("dve")
                    p.wait("dve", "dve", bcg_done[ep_])
                    i = v.tensor_tensor(
                        h_all[:, pi_ * 1024 + th_ * 512: pi_ * 1024 + (th_ + 1) * 512],
                        t1[:, up * 512:(up + 1) * 512],
                        bcg_sb[:, ep_ * 1024 + th_ * 512: ep_ * 1024 + (th_ + 1) * 512],
                        ALU.mult)
                    h_done[ui] = p.tag(i, "dve")

                w2_tok = [0] * NDC

                def w2_load(m):
                    i = g.dma_start(
                        w2m[:, (m % 2) * 4096:(m % 2 + 1) * 4096]
                        .rearrange("p (e c n) -> p e c n", e=NE, c=NFC),
                        w2[:, :, m * 128:(m + 1) * 128]
                        .rearrange("e (c p) n -> p e c n", p=128))
                    w2_tok[m] = p.dtag(i, f"d_w2{m % 2}")

                deferred_t1h = []
                for e in range(NE):
                    ep = e % 2
                    if 0 < e < NE - 1:
                        ge_load(e + 1)
                    if e == NE - 2:
                        w2_load(0)
                        w2_load(1)
                    for fc in range(NFC):
                        pi = e * NFC + fc
                        sl = pi % NPC
                        p.wait("pe", f"d_w13{sl}", w13_tok[pi])
                        last_x13 = 0
                        for th in range(2):
                            up = uidx % 2
                            if uidx >= 2:
                                p.wait("pe", "dve", h_done[uidx - 2])
                            for dc in range(NDC):
                                if e == 0:
                                    p.wait("pe", "act", ln1.out_cast[dc])
                                t.matmul(x1_ps[:, up * 512:(up + 1) * 512],
                                         w1_sb[:, sl * PSZ + dc * 128:(sl * PSZ + (dc + 1) * 128)],
                                         xnew[:, dc * 1024 + th * 512: dc * 1024 + (th + 1) * 512],
                                         start=(dc == 0), stop=(dc == NDC - 1))
                                i = t.matmul(x3_ps[:, up * 512:(up + 1) * 512],
                                             w3_sb[:, sl * PSZ + dc * 128:(sl * PSZ + (dc + 1) * 128)],
                                             xnew[:, dc * 1024 + th * 512: dc * 1024 + (th + 1) * 512],
                                             start=(dc == 0), stop=(dc == NDC - 1))
                            x13_tok = p.tag(i, "pe")
                            last_x13 = x13_tok
                            p.wait("act", "pe", x13_tok)
                            if uidx >= 2:
                                p.wait("act", "dve", h_done[uidx - 2])
                            i = sc.activation(sil[:, up * 512:(up + 1) * 512],
                                              x1_ps[:, up * 512:(up + 1) * 512], AF.Silu)
                            sil_tok = p.tag(i, "act")
                            if e == 0 and fc == 0:
                                deferred_t1h.append((uidx, up, pi, th, sil_tok, x13_tok))
                            else:
                                emit_t1h(uidx, up, pi, th, sil_tok, x13_tok)
                            uidx += 1
                        if e == 0 and fc == 0:
                            gates_finish()
                            ge_load(0)
                            ge_load(1)
                            bcg_make(0)
                            for args in deferred_t1h:
                                emit_t1h(*args)
                        if pi + NPC < NE * NFC:
                            p.wait("pool", "pe", last_x13)
                            w13_load(pi + NPC)
                        if fc == 1 and e + 1 < NE:
                            bcg_make(e + 1)
            p.barrier(exclude=("d_w20", "d_w21"))
            w13s.close()
            ln2 = LN()
            with nc.psum_tensor("mo_ps", [128, 2 * 512], F32) as mo_ps:
                madd = {}
                widx = 0
                for m in range(NDC):
                    p.wait("pe", f"d_w2{m % 2}", w2_tok[m])
                    mo_pe = 0
                    for th in range(2):
                        par = widx % 2
                        if widx >= 2:
                            p.wait("pe", "dve", madd[widx - 2])
                        for e in range(NE):
                            for fc in range(NFC):
                                key = e * NFC + fc
                                i = t.matmul(
                                    mo_ps[:, par * 512:(par + 1) * 512],
                                    w2m[:, (m % 2) * 4096 + key * 128: (m % 2) * 4096 + (key + 1) * 128],
                                    h_all[:, key * 1024 + th * 512: key * 1024 + (th + 1) * 512],
                                    start=(key == 0), stop=(key == NE * NFC - 1))
                        mo_pe = p.tag(i, "pe")
                        if th == 0 and m >= 1:
                            ln2.chunk_stats(m - 1)
                        p.wait("dve", "pe", mo_pe)
                        i = v.tensor_tensor(
                            X[:, m * 1024 + th * 512: m * 1024 + (th + 1) * 512],
                            X[:, m * 1024 + th * 512: m * 1024 + (th + 1) * 512],
                            mo_ps[:, par * 512:(par + 1) * 512], ALU.add)
                        madd[widx] = p.tag(i, "dve")
                        widx += 1
                    ln2.chunk(m, madd[widx - 1], defer_stats=True)
                    if m + 2 < NDC:
                        p.wait("pool", "pe", mo_pe)
                        w2_load(m + 2)
            ln2.chunk_stats(NDC - 1)
            dbg_store("dR2", X[:])
            ln2.finalize(apply=False)
            w2ctx.close()
        p.barrier()

        with nc.sbuf_tensor("ytile", [128, 2 * D], F32) as ytile, \
             nc.psum_tensor("yo_ps", [128, 2 * 512], F32) as yo_ps:
            ydma = {}
            ev_tok = {}
            for ot in range(NOT):
                par = ot % 2
                for grp in range(4):
                    gi4 = ot * 4 + grp
                    gp = grp % 2
                    if gi4 >= 2:
                        p.wait("pe", "dve", ev_tok[gi4 - 2])
                    for j in range(4):
                        dc = grp * 4 + j
                        p.wait("pe", "dve", ln2.x_tok[dc])
                        i = t.transpose(
                            yo_ps[:, gp * 512 + j * 128: gp * 512 + (j + 1) * 128],
                            X[:, dc * 1024 + ot * 128: dc * 1024 + (ot + 1) * 128],
                            id_f[:])
                        p.tag(i, "pe")
                    tr_tok = p.cnt["pe"]
                    p.wait("dve", "pe", tr_tok)
                    if ot >= 2 and grp == 0:
                        p.wait("dve", f"d_y{par}", ydma[ot - 2])
                    i = v.tensor_scalar(
                        ytile[:, par * D + grp * 512: par * D + (grp + 1) * 512],
                        yo_ps[:, gp * 512:(gp + 1) * 512],
                        ln2.m_col[:, ot:ot + 1], ln2.i_col[:, ot:ot + 1],
                        ALU.subtract, ALU.mult)
                    ev_tok[gi4] = p.tag(i, "dve")
                yq, yeng = (sp, "sp") if ot % 2 == 0 else (sc, "act")
                p.wait(yeng, "dve", ev_tok[ot * 4 + 3])
                i = yq.dma_start(y[ot * 128:(ot + 1) * 128, :],
                                 ytile[:, par * D:(par + 1) * D])
                ydma[ot] = p.dtag(i, f"d_y{par}")
            for sname in list(p.cnt.keys()):
                if sname.startswith("d_y") or sname == "d_dbg":
                    sp.wait_ge(p.sem[sname], p.cnt[sname])
    return nc, dbg


_nc_cache = {}


def _get_program(debug_outputs=()):
    key = tuple(tuple(x) if isinstance(x, (list, tuple)) else x for x in debug_outputs)
    if key not in _nc_cache:
        _nc_cache[key] = build_program(debug_outputs)
    return _nc_cache[key]


def make_in_maps(inputs):
    x = np.ascontiguousarray(np.asarray(inputs["L0_attn_x"], dtype=np.float32))
    ws = {
        "Wq": np.ascontiguousarray(np.asarray(inputs["L0_attn_Wq"], dtype=np.float32)),
        "Wk": np.ascontiguousarray(np.asarray(inputs["L0_attn_Wk"], dtype=np.float32)),
        "Wv": np.ascontiguousarray(np.asarray(inputs["L0_attn_Wv"], dtype=np.float32)),
        "Wo": np.ascontiguousarray(np.asarray(inputs["L0_attn_Wo"], dtype=np.float32)),
        "bq": np.ascontiguousarray(np.asarray(inputs["L0_attn_bq"], dtype=np.float32)).reshape(1, D),
        "bk": np.ascontiguousarray(np.asarray(inputs["L0_attn_bk"], dtype=np.float32)).reshape(1, 512),
        "bv": np.ascontiguousarray(np.asarray(inputs["L0_attn_bv"], dtype=np.float32)).reshape(1, 512),
        "bo": np.ascontiguousarray(np.asarray(inputs["L0_attn_bo"], dtype=np.float32)).reshape(1, D),
        "rw": np.ascontiguousarray(np.asarray(inputs["L0_ffn_router_w"], dtype=np.float32)),
        "w1": np.ascontiguousarray(np.asarray(inputs["L0_ffn_w1"], dtype=np.float32)),
        "w2": np.ascontiguousarray(np.asarray(inputs["L0_ffn_w2"], dtype=np.float32)),
        "w3": np.ascontiguousarray(np.asarray(inputs["L0_ffn_w3"], dtype=np.float32)),
    }
    in_maps = []
    for c in range(8):
        b, qi = c // 4, c % 4
        m = {"xb": x[b], "xq": np.ascontiguousarray(x[b, qi * TOK:(qi + 1) * TOK])}
        m.update(ws)
        in_maps.append(m)
    return in_maps


def kernel(**inputs):
    in_maps = make_in_maps(inputs)
    nc, _ = _get_program()
    res = run_bass_kernel_spmd(nc, in_maps, core_ids=list(range(8)))
    out = np.empty((B, S, D), np.float32)
    for c in range(8):
        b, qi = c // 4, c % 4
        out[b, qi * TOK:(qi + 1) * TOK] = res.results[c]["y"]
    return out
